# version 01 (final)
# baseline (speedup 1.0000x reference)
import numpy as np
import ml_dtypes
from contextlib import ExitStack, contextmanager
import concourse.bass as bass
import concourse.mybir as mybir
from concourse.bass_utils import run_bass_kernel_spmd

F32 = mybir.dt.float32
BF16 = mybir.dt.bfloat16
I32 = mybir.dt.int32
AF = mybir.ActivationFunctionType
ALU = mybir.AluOpType
AX = mybir.AxisListType

D = 1024
T = 2048
NKT = D // 128
BF = ml_dtypes.bfloat16
N_CORES = 8


class _Op:
    __slots__ = ("engine", "fn", "reads", "writes", "tag", "dmaq",
                 "needs_inc", "inc_idx", "dma_idx", "deps")
    def __init__(self, engine, fn, reads, writes, tag, dmaq):
        self.engine = engine
        self.fn = fn
        self.reads = set(reads)
        self.writes = set(writes)
        self.tag = tag
        self.dmaq = dmaq
        self.needs_inc = False
        self.inc_idx = None
        self.dma_idx = None
        self.deps = set()


ENGINES = ("pe", "act", "dve", "pool", "sp")
_SECT = {"pe": "tensor", "act": "scalar", "dve": "vector", "pool": "gpsimd",
         "sp": "sync"}


class Prog:
    def __init__(self, nc):
        self.nc = nc
        self.ops = []
        self.barriers = []

    def op(self, engine, fn, reads=(), writes=(), tag=""):
        assert engine in ENGINES, engine
        self.ops.append(_Op(engine, fn, reads, writes, tag, None))
        return len(self.ops) - 1

    def dma(self, queue, fn, reads=(), writes=(), tag=""):
        assert queue in ("sp", "act", "pool")
        self.ops.append(_Op(queue, fn, reads, writes, tag, queue))
        return len(self.ops) - 1

    def barrier(self):
        self.barriers.append(len(self.ops))

    def emit(self, stack):
        nc = self.nc
        ops = self.ops
        wmax = {}
        rmax = {}
        def chan(op):
            return op.engine
        for i, op in enumerate(ops):
            for k in op.reads:
                for j in wmax.get(k, {}).values():
                    op.deps.add(j)
            for k in op.writes:
                for j in wmax.get(k, {}).values():
                    op.deps.add(j)
                for j in rmax.get(k, {}).values():
                    op.deps.add(j)
            c = chan(op)
            for k in op.reads:
                rmax.setdefault(k, {})[c] = i
            for k in op.writes:
                wmax.setdefault(k, {})[c] = i
        if SERIALIZE is True:
            for i in range(1, len(ops)):
                ops[i].deps.add(i - 1)
        elif isinstance(SERIALIZE, tuple) and SERIALIZE[0] == "range":
            lo, hi = SERIALIZE[1], SERIALIZE[2]
            for i in range(max(lo, 1), min(hi, len(ops))):
                ops[i].deps.add(i - 1)
        elif SERIALIZE == "dma":
            for i, op in enumerate(ops):
                if op.dmaq is not None:
                    if i > 0:
                        op.deps.add(i - 1)
                    if i + 1 < len(ops):
                        ops[i + 1].deps.add(i)
        elif SERIALIZE == "poolq":
            prev = None
            for i, op in enumerate(ops):
                if op.dmaq == "pool":
                    if prev is not None:
                        op.deps.add(prev)
                    prev = i
        elif SERIALIZE == "pe":
            for i, op in enumerate(ops):
                if op.engine == "pe" and op.dmaq is None:
                    if i > 0:
                        op.deps.add(i - 1)
                    if i + 1 < len(ops):
                        ops[i + 1].deps.add(i)
        for bidx in self.barriers:
            snap = {}
            for j in range(bidx - 1, -1, -1):
                c = ops[j].engine + ("q" if ops[j].dmaq else "")
                if c not in snap:
                    snap[c] = j
            seen = set()
            for j in range(bidx, len(ops)):
                c = ops[j].engine
                if c not in seen:
                    seen.add(c)
                    for s in snap.values():
                        ops[j].deps.add(s)
        for i, op in enumerate(ops):
            pruned = set()
            for d_ in op.deps:
                if d_ >= i:
                    continue
                p = ops[d_]
                if p.dmaq is not None:
                    pruned.add(d_)
                elif p.engine != op.engine or (SYNC_SAME_ENGINE
                                               and p.engine != "pe"):
                    p.needs_inc = True
                    pruned.add(d_)
            op.deps = pruned
        NSLOT = 8
        inc_count = {e: 0 for e in ENGINES}
        dma_count = {q: 0 for q in ("sp", "act", "pool")}
        for op in ops:
            if op.dmaq is not None:
                op.dma_idx = dma_count[op.dmaq]
                dma_count[op.dmaq] += 1
            if op.needs_inc and op.dmaq is None:
                inc_count[op.engine] += 1
                op.inc_idx = inc_count[op.engine]
        esem = {e: stack.enter_context(nc.semaphore(f"sem_{e}"))
                for e in ENGINES if inc_count[e] > 0}
        qsem = {}
        for q in ("sp", "act", "pool"):
            if dma_count[q] > 0:
                qsem[q] = [stack.enter_context(nc.semaphore(f"qs_{q}{i}"))
                           for i in range(NSLOT)]
        per_engine = {e: [] for e in ENGINES}
        for i, op in enumerate(ops):
            per_engine[op.engine].append(i)
        block = stack.enter_context(nc.Block())
        watermark = {}

        def make_section(e):
            idxs = per_engine[e]
            def section(eng):
                for i in idxs:
                    op = ops[i]
                    waits = {}
                    for d_ in op.deps:
                        p = ops[d_]
                        if p.dmaq is not None:
                            key = ("q", p.dmaq, p.dma_idx % NSLOT)
                            val = 16 * (p.dma_idx // NSLOT + 1)
                        else:
                            key = ("e", p.engine)
                            val = p.inc_idx
                        if waits.get(key, 0) < val:
                            waits[key] = val
                    if op.dmaq is not None and op.dma_idx >= NSLOT:
                        key = ("q", op.dmaq, op.dma_idx % NSLOT)
                        val = 16 * (op.dma_idx // NSLOT)
                        if waits.get(key, 0) < val:
                            waits[key] = val
                    for key, val in sorted(waits.items(), key=lambda kv: str(kv[0])):
                        wm = watermark.get((e, key), 0)
                        if val <= wm:
                            continue
                        watermark[(e, key)] = val
                        sem = qsem[key[1]][key[2]] if key[0] == "q" else esem[key[1]]
                        eng.wait_ge(sem, val)
                    ins = op.fn(eng)
                    if op.dmaq is not None:
                        assert ins is not None, op.tag
                        ins.then_inc(qsem[op.dmaq][op.dma_idx % NSLOT], 16)
                    elif op.needs_inc:
                        assert ins is not None, op.tag
                        ins.then_inc(esem[op.engine], 1)
            return section

        for e in ENGINES:
            if per_engine[e]:
                getattr(block, _SECT[e])(make_section(e))
        return {"inc": inc_count, "dma": dma_count, "nops": len(ops)}


DEBUG_DUMPS = []
PARANOID = False
L0_TRUNC = 99
SERIALIZE = False
SYNC_SAME_ENGINE = True

class Ctx:
    def __init__(self, nc, prog, stack):
        self.nc = nc
        self.prog = prog
        self.stack = stack
        self.uid = 0
        self.psum_banks = []
        self._bank_rr = 0
        self._scopes = []

    def dump(self, name, ap, keys, shape=None, dt=None):
        if name not in DEBUG_DUMPS:
            return
        shape = shape or list(ap.shape)
        dt = dt or ap.dtype
        o = self.nc.dram_tensor(f"dbg_{name}", list(shape), dt,
                                kind="ExternalOutput").ap()
        def f(eng, o=o, ap=ap):
            return eng.dma_start(out=o[:, :], in_=ap)
        self.prog.dma("sp", f, reads=set(keys), writes={f"dbg_{name}"},
                      tag=f"dump_{name}")

    def fresh(self, prefix="t"):
        self.uid += 1
        return f"{prefix}_{self.uid}"

    def sb(self, shape, dt=F32, name=None):
        name = name or self.fresh("sb")
        st = self._scopes[-1] if self._scopes else self.stack
        return st.enter_context(self.nc.sbuf_tensor(name, shape, dt))

    @contextmanager
    def scope(self):
        st = ExitStack()
        self._scopes.append(st)
        try:
            yield
        finally:
            self._scopes.pop()
            st.close()
            self.prog.barrier()

    def init_psum(self):
        for i in range(6):
            t = self.stack.enter_context(self.nc.psum_tensor(f"pbf{i}", [128, 512], F32))
            self.psum_banks.append((t, f"pb{i}"))
        self.psum_bf = []
        for i in range(2):
            t = self.stack.enter_context(self.nc.psum_tensor(f"pbb{i}", [128, 1024], BF16))
            self.psum_bf.append((t, f"pbb{i}"))
        self._bf_rr = 0

    def take_bank(self):
        t, k = self.psum_banks[self._bank_rr % len(self.psum_banks)]
        self._bank_rr += 1
        return t, k

    def take_bf_bank(self):
        t, k = self.psum_bf[self._bf_rr % 2]
        self._bf_rr += 1
        return t, k


BLOCKS = {}


class PlainView:
    def __init__(self, sb_, stride=T):
        self.sb_ = sb_
        self.stride = stride
    def slice(self, kt, t0, tw):
        return self.sb_[:, kt * self.stride + t0:kt * self.stride + t0 + tw]


class ShiftedView:
    def __init__(self, sb_):
        self.sb_ = sb_
    def slice(self, kt, t0, tw):
        base = kt * (T + 1) + 1 + t0
        return self.sb_[:, base:base + tw]


def emit_linear(ctx, K, O, w_hbm, act_ap, act_keys, evac, tag,
                tw=512, ntok=T, o0=0, ot_size=128, extra_mm=None, extra_keys=(),
                w_sb=None, pre_chunk=None):
    p = ctx.prog
    nkt = K // 128
    n_tc = (ntok + tw - 1) // tw
    n_ot = (O + ot_size - 1) // ot_size
    slots = None
    if w_sb is None:
        assert nkt * ot_size <= 4096, (nkt, ot_size)
        slots = ctx.wslots
    for ot in range(n_ot):
        ow = min(ot_size, O - ot * ot_size)
        oabs = o0 + ot * ot_size
        if w_sb is None:
            wt, wkey = slots[ot % len(slots)]
            oti = (o0 // ot_size) + ot
            def ldfn(eng, wt=wt, oti=oti, ow=ow):
                return eng.dma_start(out=wt[:, 0:nkt * ot_size],
                                     in_=w_hbm[oti, :, :])
            p.dma("pool", ldfn, reads=set(), writes={wkey}, tag=f"w_{tag}{ot}")
            def lhsT_of(kt, wt=wt, ow=ow):
                return wt[:, kt * ot_size:kt * ot_size + ow]
        else:
            wt, wkey = w_sb
            def lhsT_of(kt, wt=wt, oabs=oabs, ow=ow):
                Otot = wt.shape[1] // nkt
                return wt[:, kt * Otot + oabs:kt * Otot + oabs + ow]
        if pre_chunk is not None:
            pc_ap, pc_keys, pc_w, pc_evac = pre_chunk
            bank, bk = ctx.take_bank()
            papc = bank[0:ow, 0:pc_w]
            for kt in range(nkt):
                def mmpc(eng, kt=kt, papc=papc, lf=lhsT_of):
                    return eng.matmul(papc, lf(kt), pc_ap(kt),
                                      start=(kt == 0), stop=(kt == nkt - 1))
                p.op("pe", mmpc, reads=pc_keys(kt) | {wkey}, writes={bk},
                     tag=f"mmpc_{tag}{ot}")
            pc_evac(oabs, papc, bk)
        for tc in range(n_tc):
            t0 = tc * tw
            tw_ = min(tw, ntok - t0)
            bank, bk = ctx.take_bank()
            pap = bank[0:ow, 0:tw_]
            for kt in range(nkt):
                last = (kt == nkt - 1) and (extra_mm is None)
                def mmfn(eng, kt=kt, t0=t0, tw_=tw_, pap=pap, last=last, lf=lhsT_of):
                    return eng.matmul(pap, lf(kt), act_ap(kt, t0, tw_),
                                      start=(kt == 0), stop=last)
                p.op("pe", mmfn, reads=act_keys(kt) | {wkey}, writes={bk},
                     tag=f"mm_{tag}{ot}_{tc}_{kt}")
            if extra_mm is not None:
                def emfn(eng, oabs=oabs, t0=t0, tw_=tw_, pap=pap):
                    return extra_mm(eng, oabs, t0, tw_, pap)
                p.op("pe", emfn, reads=set(extra_keys), writes={bk}, tag=f"mmx_{tag}")
            evac(oabs, t0, tw_, pap, bk)


def xkeys(t0, tw):
    return {f"x_t{c}" for c in range(t0 // 512, (t0 + tw - 1) // 512 + 1)}


def emit_ln(ctx, out_view, out_key, tag, src_view=None, src_key=None,
            tw=512, eps=1e-5, gain_bias=None, gb_key=None, extra_view=None,
            extra_key=None, resid_echo=False):
    p = ctx.prog
    nchunk = T // tw
    mrow = ctx.rowm; mk = "rowm"
    vrow = ctx.rowv; vk = "rowv"
    rrow = ctx.rowv; rk = "rowv"
    hbm_src = src_view is None
    _lnstack = ExitStack()
    ctx._scopes.append(_lnstack)
    x2 = ctx.sb([128, NKT * tw], BF16)
    x2k = ctx.fresh(f"lnx2_{tag}")
    if hbm_src:
        xc = ctx.sb([128, NKT * tw], F32)
        xck = ctx.fresh(f"lnxc_{tag}")
        xr_hbm = ctx.x_hbm.rearrange("(kt p) t -> p kt t", p=128)
    ctx._scopes.pop()
    ctx._lnxc_stack = getattr(ctx, "_lnxc_stack", [])
    ctx._lnxc_stack.append(_lnstack)
    for c in range(nchunk):
        t0 = c * tw
        if hbm_src:
            def fload(eng, t0=t0):
                return eng.dma_start(
                    out=xc[:, :].rearrange("p (kt t) -> p kt t", kt=NKT),
                    in_=xr_hbm[:, :, t0:t0 + tw])
            p.dma("sp", fload, reads=xkeys(t0, tw), writes={xck}, tag=f"lnld_{tag}{c}")
            sv = PlainView(xc, stride=tw)
            svoff = 0
            skey = xck
        else:
            sv = src_view
            svoff = t0
            skey = src_key
        for kt in range(NKT):
            def sqfn(eng, kt=kt, sv=sv, o=svoff):
                return eng.activation(x2[:, kt * tw:kt * tw + tw],
                                      sv.slice(kt, o, tw), AF.Square)
            p.op("act", sqfn, reads={skey}, writes={x2k}, tag=f"sq_{tag}{kt}")
        b1, k1 = ctx.take_bank()
        b2, k2 = ctx.take_bank()
        s1ap = b1[0:1, 0:tw]; s2ap = b2[0:1, 0:tw]
        for kt in range(NKT):
            def f1(eng, kt=kt, s1ap=s1ap, sv=sv, o=svoff):
                rhs = sv.slice(kt, o, tw)
                onesc = (ctx.onesc_f32 if rhs.dtype == F32 else ctx.onesc_bf)
                return eng.matmul(s1ap, onesc[:, :], rhs,
                                  start=(kt == 0), stop=(kt == NKT - 1))
            p.op("pe", f1, reads={skey, "ones1"}, writes={k1}, tag=f"s1_{tag}")
        for kt in range(NKT):
            def f2(eng, kt=kt, s2ap=s2ap):
                return eng.matmul(s2ap, ctx.onesc_bf[:, :],
                                  x2[:, kt * tw:kt * tw + tw],
                                  start=(kt == 0), stop=(kt == NKT - 1))
            p.op("pe", f2, reads={x2k, "ones1"}, writes={k2}, tag=f"s2_{tag}")
        def fm(eng, s1ap=s1ap, t0=t0):
            return eng.tensor_single_scalar(mrow[:, t0:t0 + tw], s1ap, 1.0 / D, ALU.mult)
        p.op("dve", fm, reads={k1}, writes={mk}, tag=f"m_{tag}")
        def fm2(eng, t0=t0):
            return eng.tensor_tensor(vrow[:, t0:t0 + tw], mrow[:, t0:t0 + tw],
                                     mrow[:, t0:t0 + tw], ALU.mult)
        p.op("dve", fm2, reads={mk}, writes={vk}, tag=f"m2_{tag}")
        def fv(eng, s2ap=s2ap, t0=t0):
            return eng.scalar_tensor_tensor(vrow[:, t0:t0 + tw], s2ap, 1.0 / D,
                                            vrow[:, t0:t0 + tw], ALU.mult, ALU.subtract)
        p.op("dve", fv, reads={k2, vk}, writes={vk}, tag=f"v_{tag}")
        def fl(eng, t0=t0):
            return eng.activation(vrow[:, t0:t0 + tw], vrow[:, t0:t0 + tw], AF.Ln,
                                  bias=float(eps))
        p.op("act", fl, reads={vk, "consts"}, writes={vk}, tag=f"ln_{tag}")
        def fe(eng, t0=t0):
            return eng.activation(rrow[:, t0:t0 + tw], vrow[:, t0:t0 + tw], AF.Exp,
                                  scale=-0.5)
        p.op("act", fe, reads={vk}, writes={rk}, tag=f"rstd_{tag}")
        bm, bmk = ctx.take_bank()
        br, brk = ctx.take_bank()
        def fbm(eng, bm=bm, t0=t0):
            return eng.matmul(bm[:, 0:tw], ctx.ones1_f32[:, :], mrow[:, t0:t0 + tw],
                              start=True, stop=True)
        p.op("pe", fbm, reads={mk, "ones1"}, writes={bmk}, tag=f"bm_{tag}")
        def fbr(eng, br=br, t0=t0):
            return eng.matmul(br[:, 0:tw], ctx.ones1_f32[:, :], rrow[:, t0:t0 + tw],
                              start=True, stop=True)
        p.op("pe", fbr, reads={rk, "ones1"}, writes={brk}, tag=f"br_{tag}")
        for kt in range(NKT):
            sub = ctx.scratch_f32
            def fsub(eng, kt=kt, bm=bm, sub=sub, sv=sv, o=svoff):
                return eng.tensor_tensor(sub[:, 0:tw], sv.slice(kt, o, tw),
                                         bm[:, 0:tw], ALU.subtract)
            p.op("dve", fsub, reads={skey, bmk}, writes={"scr32"}, tag=f"sub_{tag}")
            if resid_echo:
                def fmulf(eng, br=br, sub=sub):
                    return eng.tensor_tensor(sub[:, 0:tw], sub[:, 0:tw], br[:, 0:tw],
                                             ALU.mult)
                p.op("dve", fmulf, reads={"scr32", brk}, writes={"scr32"},
                     tag=f"nmf_{tag}")
                def fecho(eng, kt=kt, t0=t0, sub=sub):
                    return eng.dma_start(out=xr_hbm[:, kt, t0:t0 + tw],
                                         in_=sub[:, 0:tw], accum_op=ALU.add)
                p.dma("pool", fecho, reads={"scr32"} | xkeys(t0, tw),
                      writes=xkeys(t0, tw), tag=f"echo_{tag}")
                def fcp(eng, kt=kt, t0=t0, sub=sub):
                    return eng.tensor_copy(out_view.slice(kt, t0, tw), sub[:, 0:tw])
                p.op("dve", fcp, reads={"scr32"}, writes={out_key}, tag=f"cp_{tag}")
            elif gain_bias is None:
                def fmul(eng, kt=kt, t0=t0, br=br, sub=sub):
                    return eng.tensor_tensor(out_view.slice(kt, t0, tw), sub[:, 0:tw],
                                             br[:, 0:tw], ALU.mult)
                p.op("dve", fmul, reads={"scr32", brk}, writes={out_key}, tag=f"nm_{tag}")
            else:
                g_cols, b_cols = gain_bias
                def fmulg(eng, br=br, sub=sub):
                    return eng.tensor_tensor(sub[:, 0:tw], sub[:, 0:tw], br[:, 0:tw],
                                             ALU.mult)
                p.op("dve", fmulg, reads={"scr32", brk}, writes={"scr32"},
                     tag=f"nmg_{tag}")
                def fgb(eng, kt=kt, t0=t0, sub=sub):
                    return eng.tensor_scalar(out_view.slice(kt, t0, tw), sub[:, 0:tw],
                                             g_cols[:, kt:kt + 1], b_cols[:, kt:kt + 1],
                                             ALU.mult, ALU.add)
                p.op("dve", fgb, reads={"scr32", gb_key}, writes={out_key},
                     tag=f"gb_{tag}")
            if extra_view is not None:
                def fmul2(eng, kt=kt, t0=t0, br=br, sub=sub):
                    return eng.tensor_tensor(extra_view.slice(kt, t0, tw), sub[:, 0:tw],
                                             br[:, 0:tw], ALU.mult)
                p.op("dve", fmul2, reads={"scr32", brk}, writes={extra_key},
                     tag=f"nm2_{tag}")
    ctx._lnxc_stack.pop().close()
    ctx.prog.barrier()
    return mrow, mk, rrow, rk


def emit_resid_accum(ctx, sb_ap, keys, kt, t0, tw, tag):
    xr_hbm = ctx.x_hbm.rearrange("(kt p) t -> p kt t", p=128)
    def f(eng, kt=kt, t0=t0, tw=tw, sb_ap=sb_ap):
        return eng.dma_start(out=xr_hbm[:, kt, t0:t0 + tw], in_=sb_ap,
                             accum_op=ALU.add)
    ctx.prog.dma("pool", f, reads=set(keys) | xkeys(t0, tw), writes=xkeys(t0, tw),
                 tag=f"racc_{tag}")


def prep_wtiled(W, ot_size=128):
    K_, O_ = W.shape
    nkt = K_ // 128
    n_ot = (O_ + ot_size - 1) // ot_size
    Opad = n_ot * ot_size
    if Opad != O_:
        W = np.concatenate([W, np.zeros((K_, Opad - O_), W.dtype)], 1)
    Wr = W.reshape(nkt, 128, n_ot, ot_size).transpose(2, 1, 0, 3)
    return np.ascontiguousarray(Wr.reshape(n_ot, 128, nkt * ot_size))


def prep_cols(arr):
    n = arr.shape[0] // 128
    return np.ascontiguousarray(arr.reshape(n, 128).T)


def load_cols(ctx, hbm_ap, n, tag, dt=F32, key=None):
    t = ctx.sb([128, n], dt)
    k = key or ctx.fresh(f"{tag}k")
    def f(eng):
        return eng.dma_start(out=t[:, :], in_=hbm_ap[:, :])
    ctx.prog.dma("sp", f, reads=set(), writes={k}, tag=f"ld_{tag}")
    return t, k



def emit_channel_mix(ctx, W, tag):
    p = ctx.prog
    with ctx.scope():
        TP = T + 1
        xn = ctx.sb([128, NKT * TP], BF16)
        xnk = ctx.fresh(f"xn{tag}")
        def fz(eng):
            return eng.memset(
                xn[:, :].rearrange("p (kt t) -> p kt t", kt=NKT)[:, :, 0:1], 0.0)
        p.op("dve", fz, reads=set(), writes={xnk}, tag=f"z_{tag}")
        emit_ln(ctx, ShiftedView(xn), xnk, f"ln{tag}")
        mu_r, murk = load_cols(ctx, W["mu_r"], NKT, f"mur{tag}")
        mu_k, mukk = load_cols(ctx, W["mu_k"], NKT, f"muk{tag}")
        om_r = ctx.sb([128, NKT]); om_k = ctx.sb([128, NKT])
        omk = ctx.fresh(f"om{tag}")
        for src_, srk, dst in ((mu_r, murk, om_r), (mu_k, mukk, om_k)):
            def f(eng, src_=src_, dst=dst):
                return eng.tensor_scalar(dst[:, :], src_[:, :], -1.0, 1.0,
                                         ALU.mult, ALU.add)
            p.op("dve", f, reads={srk, "consts"}, writes={omk}, tag=f"om_{tag}")
        xr = ctx.sb([128, NKT * T], BF16); xrk = ctx.fresh(f"xr{tag}")
        def mix(mu_t, muk_, om_t, dst, dstk, tagx):
            for kt in range(NKT):
                cur = xn[:, kt * TP + 1:kt * TP + 1 + T]
                prv = xn[:, kt * TP:kt * TP + T]
                tmp = ctx.scratch_bf
                def f1(eng, cur=cur, mu_t=mu_t, kt=kt, tmp=tmp):
                    return eng.tensor_scalar(tmp[:, 0:T], cur, mu_t[:, kt:kt + 1],
                                             None, ALU.mult)
                p.op("dve", f1, reads={xnk, muk_}, writes={"scrbf"}, tag=f"mx1_{tagx}")
                def f2(eng, prv=prv, om_t=om_t, kt=kt, dst=dst, tmp=tmp):
                    return eng.scalar_tensor_tensor(dst[:, kt * T:kt * T + T], prv,
                                                    om_t[:, kt:kt + 1], tmp[:, 0:T],
                                                    ALU.mult, ALU.add)
                p.op("dve", f2, reads={xnk, omk, "scrbf"}, writes={dstk},
                     tag=f"mx2_{tagx}")
        mix(mu_r, murk, om_r, xr, xrk, f"r{tag}")
        xx = ctx.sb([128, NKT * T], BF16); xxk = ctx.fresh(f"xx{tag}")
        mix(mu_k, mukk, om_k, xx, xxk, f"k{tag}")
        r_sb, rk = xn, xnk
        def evac_r(oabs, t0, tw, pap, bk):
            ot = oabs // 128
            def f(eng, ot=ot, t0=t0, tw=tw, pap=pap):
                return eng.activation(r_sb[:, ot * T + t0:ot * T + t0 + tw], pap,
                                      AF.Sigmoid)
            p.op("act", f, reads={bk}, writes={rk}, tag=f"evr_{tag}")
        emit_linear(ctx, D, D, W["W_r"],
                    lambda kt, t0, tw: xr[:, kt * T + t0:kt * T + t0 + tw],
                    lambda kt: {xrk}, evac_r, f"wr{tag}")
        k2_sb, k2k = xr, xrk
        def evac_k(oabs, t0, tw, pap, bk):
            ot = oabs // 128
            def f(eng, tw=tw, pap=pap):
                return eng.activation(ctx.scratch_bf[:, 0:tw], pap, AF.Relu)
            p.op("act", f, reads={bk}, writes={"scrbf"}, tag=f"evk_{tag}")
            def f2(eng, ot=ot, t0=t0, tw=tw):
                return eng.tensor_tensor(k2_sb[:, ot * T + t0:ot * T + t0 + tw],
                                         ctx.scratch_bf[:, 0:tw],
                                         ctx.scratch_bf[:, 0:tw], ALU.mult)
            p.op("dve", f2, reads={"scrbf"}, writes={k2k}, tag=f"evk2_{tag}")
        emit_linear(ctx, D, D, W["W_k"],
                    lambda kt, t0, tw: xx[:, kt * T + t0:kt * T + t0 + tw],
                    lambda kt: {xxk}, evac_k, f"wk{tag}")
        outc = ctx.sb([128, 512], F32); outck = ctx.fresh(f"oc{tag}")
        def evac_v(oabs, t0, tw, pap, bk):
            ot = oabs // 128
            def f(eng, ot=ot, t0=t0, tw=tw, pap=pap):
                return eng.tensor_tensor(outc[:, 0:tw], pap,
                                         r_sb[:, ot * T + t0:ot * T + t0 + tw], ALU.mult)
            p.op("dve", f, reads={bk, rk}, writes={outck}, tag=f"evv_{tag}")
            emit_resid_accum(ctx, outc[:, 0:tw], {outck}, ot, t0, tw, tag)
        emit_linear(ctx, D, D, W["W_v"],
                    lambda kt, t0, tw: k2_sb[:, kt * T + t0:kt * T + t0 + tw],
                    lambda kt: {k2k}, evac_v, f"wv{tag}")



def emit_l2_attn(ctx, W, tag="l2a"):
    p = ctx.prog
    with ctx.scope():
        TP = T + 1
        xn = ctx.sb([128, NKT * TP], BF16)
        xnk = ctx.fresh(f"xn{tag}")
        def fz(eng):
            return eng.memset(
                xn[:, :].rearrange("p (kt t) -> p kt t", kt=NKT)[:, :, 0:1], 0.0)
        p.op("dve", fz, reads=set(), writes={xnk}, tag=f"z_{tag}")
        emit_ln(ctx, ShiftedView(xn), xnk, f"ln{tag}")
        tmk, tmkk = load_cols(ctx, W["time_mix_k"], NKT, f"tmk{tag}")
        tmv, tmvk = load_cols(ctx, W["time_mix_v"], NKT, f"tmv{tag}")
        tmr, tmrk = load_cols(ctx, W["time_mix_r"], NKT, f"tmr{tag}")
        td, tdk = load_cols(ctx, W["time_decay"], NKT, f"td{tag}")
        tf, tfk = load_cols(ctx, W["time_first"], NKT, f"tf{tag}")
        dec = ctx.sb([128, NKT]); deck = ctx.fresh(f"dec{tag}")
        def fd1(eng):
            return eng.activation(dec[:, :], td[:, :], AF.Exp)
        p.op("act", fd1, reads={tdk}, writes={deck}, tag=f"d1_{tag}")
        def fd2(eng):
            return eng.activation(dec[:, :], dec[:, :], AF.Exp, scale=-1.0)
        p.op("act", fd2, reads={deck}, writes={deck}, tag=f"d2_{tag}")
        om = ctx.sb([128, 3 * NKT]); omk = ctx.fresh(f"om{tag}")
        for i, (src_, srk) in enumerate(((tmk, tmkk), (tmv, tmvk), (tmr, tmrk))):
            def f(eng, src_=src_, i=i):
                return eng.tensor_scalar(om[:, i * NKT:(i + 1) * NKT], src_[:, :],
                                         -1.0, 1.0, ALU.mult, ALU.add)
            p.op("dve", f, reads={srk, "consts"}, writes={omk}, tag=f"om_{tag}")
        bufA = ctx.sb([128, NKT * T], BF16); bak = ctx.fresh(f"ba{tag}")
        bufB = ctx.sb([128, NKT * T], BF16); bbk = ctx.fresh(f"bb{tag}")
        bufC = ctx.sb([128, NKT * T], BF16); bck = ctx.fresh(f"bc{tag}")
        def mix(mu_t, muk_, omoff, dst, dstk, tagx):
            for kt in range(NKT):
                cur = xn[:, kt * TP + 1:kt * TP + 1 + T]
                prv = xn[:, kt * TP:kt * TP + T]
                tmp = ctx.scratch_bf
                def f1(eng, cur=cur, mu_t=mu_t, kt=kt, tmp=tmp):
                    return eng.tensor_scalar(tmp[:, 0:T], cur, mu_t[:, kt:kt + 1],
                                             None, ALU.mult)
                p.op("dve", f1, reads={xnk, muk_}, writes={"scrbf"}, tag=f"mx1_{tagx}")
                def f2(eng, prv=prv, kt=kt, dst=dst, tmp=tmp, omoff=omoff):
                    return eng.scalar_tensor_tensor(
                        dst[:, kt * T:kt * T + T], prv,
                        om[:, omoff * NKT + kt:omoff * NKT + kt + 1], tmp[:, 0:T],
                        ALU.mult, ALU.add)
                p.op("dve", f2, reads={xnk, omk, "scrbf"}, writes={dstk},
                     tag=f"mx2_{tagx}")
        mix(tmk, tmkk, 0, bufA, bak, f"k{tag}")
        mix(tmv, tmvk, 1, bufB, bbk, f"v{tag}")
        mix(tmr, tmrk, 2, bufC, bck, f"r{tag}")
        k_sb, kk = xn, xnk
        def evac_k(oabs, t0, tw, pap, bk):
            ot = oabs // 128
            def f(eng, ot=ot, t0=t0, tw=tw, pap=pap):
                return eng.tensor_copy(k_sb[:, ot * T + t0:ot * T + t0 + tw], pap)
            p.op("dve", f, reads={bk}, writes={kk}, tag=f"evk_{tag}")
        emit_linear(ctx, D, D, W["Wk"],
                    lambda kt, t0, tw: bufA[:, kt * T + t0:kt * T + t0 + tw],
                    lambda kt: {bak}, evac_k, f"wk{tag}")
        v_sb, vk = bufA, bak
        def evac_v(oabs, t0, tw, pap, bk):
            ot = oabs // 128
            def f(eng, ot=ot, t0=t0, tw=tw, pap=pap):
                return eng.tensor_copy(v_sb[:, ot * T + t0:ot * T + t0 + tw], pap)
            p.op("dve", f, reads={bk}, writes={vk}, tag=f"evv_{tag}")
        emit_linear(ctx, D, D, W["Wv"],
                    lambda kt, t0, tw: bufB[:, kt * T + t0:kt * T + t0 + tw],
                    lambda kt: {bbk}, evac_v, f"wv{tag}")
        r_sb, rk = bufB, bbk
        def evac_r(oabs, t0, tw, pap, bk):
            ot = oabs // 128
            def f(eng, ot=ot, t0=t0, tw=tw, pap=pap):
                return eng.activation(r_sb[:, ot * T + t0:ot * T + t0 + tw], pap,
                                      AF.Sigmoid)
            p.op("act", f, reads={bk}, writes={rk}, tag=f"evr_{tag}")
        emit_linear(ctx, D, D, W["Wr"],
                    lambda kt, t0, tw: bufC[:, kt * T + t0:kt * T + t0 + tw],
                    lambda kt: {bck}, evac_r, f"wr{tag}")
        wkvr, wkvrk = bufC, bck
        a_buf = ctx.sb([128, T + 1], F32); abk = ctx.fresh(f"ab{tag}")
        b_buf = ctx.sb([128, T + 1], F32); bbk2 = ctx.fresh(f"bb2{tag}")
        dec_t = ctx.sb([128, 512], F32); dtk2 = ctx.fresh(f"dect{tag}")
        eku = ctx.sb([128, 512], F32); ekuk = ctx.fresh(f"eku{tag}")
        ek = ctx.sb([128, 512], F32); ekk = ctx.fresh(f"ek{tag}")
        ekv = ctx.sb([128, 512], F32); ekvk = ctx.fresh(f"ekv{tag}")
        num = ctx.sb([128, 512], F32); numk = ctx.fresh(f"num{tag}")
        den = ctx.sb([128, 512], F32); denk = ctx.fresh(f"den{tag}")
        for dt in range(NKT):
            def fdec(eng, dt=dt):
                return eng.tensor_scalar(dec_t[:, :], eku[:, :], 0.0,
                                         dec[:, dt:dt + 1], ALU.mult, ALU.add)
            p.op("dve", fdec, reads={deck, "consts"}, writes={dtk2}, tag=f"dect_{tag}")
            def fz2(eng, dt=dt):
                return eng.memset(a_buf[:, 0:1], 0.0)
            p.op("dve", fz2, reads=set(), writes={abk}, tag=f"za_{tag}")
            def fz3(eng, dt=dt):
                return eng.memset(b_buf[:, 0:1], 0.0)
            p.op("dve", fz3, reads=set(), writes={bbk2}, tag=f"zb_{tag}")
            for c in range(4):
                t0 = c * 512
                ksl = k_sb[:, dt * T + t0:dt * T + t0 + 512]
                vsl = v_sb[:, dt * T + t0:dt * T + t0 + 512]
                def f1(eng, ksl=ksl, dt=dt):
                    return eng.activation(eku[:, :], ksl, AF.Exp, bias=tf[:, dt:dt + 1])
                p.op("act", f1, reads={kk, tfk}, writes={ekuk}, tag=f"eku_{tag}")
                def f2(eng, ksl=ksl):
                    return eng.activation(ek[:, :], ksl, AF.Exp)
                p.op("act", f2, reads={kk}, writes={ekk}, tag=f"ek_{tag}")
                def f3(eng, vsl=vsl):
                    return eng.tensor_tensor(ekv[:, :], ek[:, :], vsl, ALU.mult)
                p.op("dve", f3, reads={ekk, vk}, writes={ekvk}, tag=f"ekv_{tag}")
                def f4(eng, t0=t0):
                    return eng.tensor_tensor_scan(
                        a_buf[:, 1 + t0:1 + t0 + 512], dec_t[:, :], ekv[:, :],
                        a_buf[:, t0:t0 + 1], ALU.mult, ALU.add)
                p.op("dve", f4, reads={dtk2, ekvk, abk}, writes={abk}, tag=f"sca_{tag}")
                def f5(eng, t0=t0):
                    return eng.tensor_tensor_scan(
                        b_buf[:, 1 + t0:1 + t0 + 512], dec_t[:, :], ek[:, :],
                        b_buf[:, t0:t0 + 1], ALU.mult, ALU.add)
                p.op("dve", f5, reads={dtk2, ekk, bbk2}, writes={bbk2}, tag=f"scb_{tag}")
                def f6(eng, vsl=vsl, t0=t0):
                    return eng.tensor_tensor(num[:, :], eku[:, :], vsl, ALU.mult)
                p.op("dve", f6, reads={ekuk, vk}, writes={numk}, tag=f"nm1_{tag}")
                def f7(eng, t0=t0):
                    return eng.tensor_tensor(num[:, :], num[:, :],
                                             a_buf[:, t0:t0 + 512], ALU.add)
                p.op("dve", f7, reads={numk, abk}, writes={numk}, tag=f"nm2_{tag}")
                def f8(eng, t0=t0):
                    return eng.tensor_tensor(den[:, :], eku[:, :],
                                             b_buf[:, t0:t0 + 512], ALU.add)
                p.op("dve", f8, reads={ekuk, bbk2}, writes={denk}, tag=f"dn1_{tag}")
                def f9(eng):
                    return eng.tensor_single_scalar(den[:, :], den[:, :], 1e-6, ALU.max)
                p.op("dve", f9, reads={denk, "consts"}, writes={denk}, tag=f"dn2_{tag}")
                def f10(eng):
                    return eng.reciprocal(den[:, :], den[:, :])
                p.op("dve", f10, reads={denk}, writes={denk}, tag=f"rc_{tag}")
                def f11(eng):
                    return eng.tensor_tensor(num[:, :], num[:, :], den[:, :], ALU.mult)
                p.op("dve", f11, reads={numk, denk}, writes={numk}, tag=f"wk_{tag}")
                def f12(eng, dt=dt, t0=t0):
                    return eng.tensor_tensor(
                        wkvr[:, dt * T + t0:dt * T + t0 + 512], num[:, :],
                        r_sb[:, dt * T + t0:dt * T + t0 + 512], ALU.mult)
                p.op("dve", f12, reads={numk, rk}, writes={wkvrk}, tag=f"wr2_{tag}")
        outc = ctx.sb([128, 512], F32); outck = ctx.fresh(f"oc{tag}")
        def evac_o(oabs, t0, tw, pap, bk):
            ot = oabs // 128
            def f(eng, tw=tw, pap=pap):
                return eng.tensor_copy(outc[:, 0:tw], pap)
            p.op("dve", f, reads={bk}, writes={outck}, tag=f"evo_{tag}")
            emit_resid_accum(ctx, outc[:, 0:tw], {outck}, ot, t0, tw, tag)
        emit_linear(ctx, D, D, W["Wo"],
                    lambda kt, t0, tw: wkvr[:, kt * T + t0:kt * T + t0 + tw],
                    lambda kt: {wkvrk}, evac_o, f"wo{tag}")


def _wdict_l2_attn(inputs):
    L = "L2_attn"
    d = {}
    for w in ("Wk", "Wv", "Wr", "Wo"):
        d[f"{L}_{w}"] = prep_wtiled(_t(inputs[f"{L}_{w}"]))
    for v in ("time_decay", "time_first", "time_mix_k", "time_mix_v", "time_mix_r"):
        d[f"{L}_{v}"] = prep_cols(np.asarray(inputs[f"{L}_{v}"], np.float32))
    return d


def _emit_l2_attn(ctx, aps):
    L = "L2_attn"
    W = {k: aps[f"{L}_{k}"] for k in ("Wk", "Wv", "Wr", "Wo", "time_decay",
                                      "time_first", "time_mix_k", "time_mix_v",
                                      "time_mix_r")}
    emit_l2_attn(ctx, W)


BLOCKS["L2_attn"] = (_wdict_l2_attn, _emit_l2_attn)



def emit_l3_attn(ctx, W, tag="l3a"):
    p = ctx.prog
    with ctx.scope():
        xn = ctx.sb([128, NKT * T], BF16)
        xnk = ctx.fresh(f"xn{tag}")
        emit_ln(ctx, PlainView(xn), xnk, f"ln{tag}")
        Dc, Dck = load_cols(ctx, W["D"], NKT, f"D{tag}")
        arow = ctx.sb([1, 16], F32); ark = ctx.fresh(f"ar{tag}")
        def fla(eng):
            return eng.dma_start(out=arow[:, :], in_=W["A_log"][:, :])
        p.dma("sp", fla, reads=set(), writes={ark}, tag=f"lda_{tag}")
        def fa1(eng):
            return eng.activation(arow[:, :], arow[:, :], AF.Exp)
        p.op("act", fa1, reads={ark}, writes={ark}, tag=f"a1_{tag}")
        def fa2(eng):
            return eng.tensor_single_scalar(arow[:, :], arow[:, :], -1.0, ALU.mult)
        p.op("dve", fa2, reads={ark, "consts"}, writes={ark}, tag=f"a2_{tag}")
        bA, bAk = ctx.take_bank()
        def fa3(eng, bA=bA):
            return eng.matmul(bA[:, 0:16], ctx.ones1_f32[:, :], arow[:, :],
                              start=True, stop=True)
        p.op("pe", fa3, reads={ark, "ones1"}, writes={bAk}, tag=f"a3_{tag}")
        A_cols = ctx.sb([128, 16], F32); Ak = ctx.fresh(f"A{tag}")
        def fa4(eng, bA=bA):
            return eng.tensor_copy(A_cols[:, :], bA[:, 0:16])
        p.op("dve", fa4, reads={bAk}, writes={Ak}, tag=f"a4_{tag}")
        bc_sb = ctx.sb([32, T], F32); bck = ctx.fresh(f"bc{tag}")
        def evac_bc(oabs, t0, tw, pap, bk):
            def f(eng, t0=t0, tw=tw, pap=pap):
                return eng.tensor_copy(bc_sb[:, t0:t0 + tw], pap)
            p.op("dve", f, reads={bk}, writes={bck}, tag=f"evbc_{tag}")
        emit_linear(ctx, D, 32, W["Wx_bc"],
                    lambda kt, t0, tw: xn[:, kt * T + t0:kt * T + t0 + tw],
                    lambda kt: {xnk}, evac_bc, f"wbc{tag}", o0=0, ot_size=32)
        brow = ctx.sb([1, T], F32); brk2 = ctx.fresh(f"br{tag}")
        crow = ctx.sb([1, T], F32); crk2 = ctx.fresh(f"cr{tag}")
        y_full = ctx.sb([128, NKT * T], BF16); yk = ctx.fresh(f"y{tag}")
        delta_sb = ctx.sb([128, T], F32); dlk = ctx.fresh(f"dl{tag}")
        sp_a = ctx.sb([128, 512], F32); spk = ctx.fresh(f"sp{tag}")
        dx = ctx.sb([128, T], BF16); dxk = ctx.fresh(f"dx{tag}")
        dA = ctx.sb([128, 512], F32); dAk = ctx.fresh(f"dA{tag}")
        dBu = ctx.sb([128, 512], F32); dBk = ctx.fresh(f"dB{tag}")
        t1 = ctx.sb([128, 512], F32); t1k = ctx.fresh(f"t1{tag}")
        y_acc = ctx.sb([128, T], F32); yak = ctx.fresh(f"ya{tag}")
        h_buf = ctx.sb([128, T + 1], F32); hbk = ctx.fresh(f"hb{tag}")
        for ht in range(NKT):
            def evac_dl(oabs, t0, tw, pap, bk):
                def f1(eng, pap=pap):
                    return eng.activation(sp_a[:, 0:tw], pap, AF.Abs)
                p.op("act", f1, reads={bk}, writes={spk}, tag=f"sp1_{tag}")
                def f2(eng, tw=tw):
                    return eng.activation(sp_a[:, 0:tw], sp_a[:, 0:tw], AF.Exp,
                                          scale=-1.0)
                p.op("act", f2, reads={spk}, writes={spk}, tag=f"sp2_{tag}")
                def f3(eng, tw=tw):
                    return eng.activation(sp_a[:, 0:tw], sp_a[:, 0:tw], AF.Ln, bias=1.0)
                p.op("act", f3, reads={spk, "consts"}, writes={spk}, tag=f"sp3_{tag}")
                def f4(eng, t0=t0, tw=tw, pap=pap):
                    return eng.scalar_tensor_tensor(delta_sb[:, t0:t0 + tw], pap, 0.0,
                                                    sp_a[:, 0:tw], ALU.max, ALU.add)
                p.op("dve", f4, reads={bk, spk, "consts"}, writes={dlk},
                     tag=f"sp4_{tag}")
            emit_linear(ctx, D, 128, W["Wx_dl"],
                        lambda kt, t0, tw: xn[:, kt * T + t0:kt * T + t0 + tw],
                        lambda kt: {xnk}, evac_dl, f"wdl{tag}{ht}",
                        o0=ht * 128, ot_size=128)
            if ht == 0:
                ctx.dump("dl0", delta_sb[:, :], {dlk})
                ctx.dump("bc", bc_sb[:, :], {bck})
                ctx.dump("acol", A_cols[:, :], {Ak})
            for c in range(4):
                t0 = c * 512
                def fdx(eng, t0=t0, ht=ht):
                    return eng.tensor_tensor(dx[:, t0:t0 + 512],
                                             delta_sb[:, t0:t0 + 512],
                                             xn[:, ht * T + t0:ht * T + t0 + 512],
                                             ALU.mult)
                p.op("dve", fdx, reads={dlk, xnk}, writes={dxk}, tag=f"dx_{tag}")
            def fzy(eng):
                return eng.memset(y_acc[:, :], 0.0)
            p.op("dve", fzy, reads=set(), writes={yak}, tag=f"zy_{tag}")
            if L0_TRUNC == 51:
                continue
            for n in range(16 if L0_TRUNC != 52 else 1):
                if ht == 0:
                    pass
                def fbr_(eng, n=n):
                    return eng.dma_start(out=brow[:, :], in_=bc_sb[n:n + 1, :])
                p.dma("sp", fbr_, reads={bck}, writes={brk2}, tag=f"xb_{tag}")
                def fcr_(eng, n=n):
                    return eng.dma_start(out=crow[:, :], in_=bc_sb[16 + n:17 + n, :])
                p.dma("sp", fcr_, reads={bck}, writes={crk2}, tag=f"xc_{tag}")
                def fzh(eng):
                    return eng.memset(h_buf[:, 0:1], 0.0)
                p.op("dve", fzh, reads=set(), writes={hbk}, tag=f"zh_{tag}")
                for c in range(4):
                    t0 = c * 512
                    bB, bBk = ctx.take_bank()
                    bC, bCk = ctx.take_bank()
                    def fb1(eng, bB=bB, t0=t0):
                        return eng.matmul(bB[:, 0:512], ctx.ones1_f32[:, :],
                                          brow[:, t0:t0 + 512], start=True, stop=True)
                    p.op("pe", fb1, reads={brk2, "ones1"}, writes={bBk}, tag=f"bB_{tag}")
                    def fb2(eng, bC=bC, t0=t0):
                        return eng.matmul(bC[:, 0:512], ctx.ones1_f32[:, :],
                                          crow[:, t0:t0 + 512], start=True, stop=True)
                    p.op("pe", fb2, reads={crk2, "ones1"}, writes={bCk}, tag=f"bC_{tag}")
                    def fda(eng, t0=t0, n=n):
                        return eng.activation(dA[:, :], delta_sb[:, t0:t0 + 512],
                                              AF.Exp, scale=A_cols[:, n:n + 1])
                    p.op("act", fda, reads={dlk, Ak}, writes={dAk}, tag=f"dA_{tag}")
                    def fdb(eng, t0=t0, bB=bB):
                        return eng.tensor_tensor(dBu[:, :], dx[:, t0:t0 + 512],
                                                 bB[:, 0:512], ALU.mult)
                    p.op("dve", fdb, reads={dxk, bBk}, writes={dBk}, tag=f"dBu_{tag}")
                    def fsc(eng, t0=t0):
                        return eng.tensor_tensor_scan(h_buf[:, 1 + t0:513 + t0],
                                                      dA[:, :], dBu[:, :],
                                                      h_buf[:, t0:t0 + 1],
                                                      ALU.mult, ALU.add)
                    p.op("dve", fsc, reads={dAk, dBk, hbk}, writes={hbk},
                         tag=f"sc_{tag}")
                    if ht == 0 and n == 0 and c == 0:
                        ctx.dump("dA00", dA[:, :], {dAk})
                        ctx.dump("dBu00", dBu[:, :], {dBk})
                        ctx.dump("h00", h_buf[:, 0:1025], {hbk})
                    if ht == 0 and n == 0 and c == 1:
                        ctx.dump("dA01", dA[:, :], {dAk})
                        ctx.dump("dBu01", dBu[:, :], {dBk})
                        ctx.dump("h01", h_buf[:, 0:1025], {hbk})
                    def fcm(eng, bC=bC, t0=t0):
                        return eng.tensor_tensor(t1[:, :], h_buf[:, 1 + t0:513 + t0],
                                                 bC[:, 0:512], ALU.mult)
                    p.op("dve", fcm, reads={hbk, bCk}, writes={t1k}, tag=f"cm_{tag}")
                    def fya(eng, t0=t0):
                        return eng.tensor_tensor(y_acc[:, t0:t0 + 512],
                                                 y_acc[:, t0:t0 + 512], t1[:, :],
                                                 ALU.add)
                    p.op("dve", fya, reads={yak, t1k}, writes={yak}, tag=f"ya_{tag}")
                    if PARANOID:
                        p.barrier()
            for c in range(4):
                t0 = c * 512
                def fyf(eng, t0=t0, ht=ht):
                    return eng.scalar_tensor_tensor(
                        y_full[:, ht * T + t0:ht * T + t0 + 512],
                        xn[:, ht * T + t0:ht * T + t0 + 512], Dc[:, ht:ht + 1],
                        y_acc[:, t0:t0 + 512], ALU.mult, ALU.add)
                p.op("dve", fyf, reads={xnk, Dck, yak}, writes={yk}, tag=f"yf_{tag}")
        ctx.dump("yf", y_full[:, :], {yk})
        outc = ctx.sb([128, 512], F32); outck = ctx.fresh(f"oc{tag}")
        def evac_o(oabs, t0, tw, pap, bk):
            ot = oabs // 128
            def f(eng, tw=tw, pap=pap):
                return eng.tensor_copy(outc[:, 0:tw], pap)
            p.op("dve", f, reads={bk}, writes={outck}, tag=f"evo_{tag}")
            emit_resid_accum(ctx, outc[:, 0:tw], {outck}, ot, t0, tw, tag)
        emit_linear(ctx, D, D, W["Wout"],
                    lambda kt, t0, tw: y_full[:, kt * T + t0:kt * T + t0 + tw],
                    lambda kt: {yk}, evac_o, f"wo{tag}")


def _wdict_l3_attn(inputs):
    L = "L3_attn"
    WxT = _t(inputs[f"{L}_Wx"])
    return {
        f"{L}_Wx_bc": prep_wtiled(np.ascontiguousarray(WxT[:, 0:32]), 32),
        f"{L}_Wx_dl": prep_wtiled(np.ascontiguousarray(WxT[:, 32:1056]), 128),
        f"{L}_Wout": prep_wtiled(_t(inputs[f"{L}_Wout"])),
        f"{L}_A_log": np.ascontiguousarray(
            np.asarray(inputs[f"{L}_A_log"], np.float32)[None, :]),
        f"{L}_D": prep_cols(np.asarray(inputs[f"{L}_D"], np.float32)),
    }


def _emit_l3_attn(ctx, aps):
    L = "L3_attn"
    W = {k: aps[f"{L}_{k}"] for k in ("Wx_bc", "Wx_dl", "Wout", "A_log", "D")}
    emit_l3_attn(ctx, W)


BLOCKS["L3_attn"] = (_wdict_l3_attn, _emit_l3_attn)



def emit_l3_ffn(ctx, W, tag="l3f"):
    p = ctx.prog
    TH = T // 2
    with ctx.scope():
        xn = ctx.sb([128, NKT * T], BF16)
        xnk = ctx.fresh(f"xn{tag}")
        emit_ln(ctx, PlainView(xn), xnk, f"ln{tag}")
        b1c, b1k = load_cols(ctx, W["b1c"], 64, f"b1{tag}")
        b2c, b2k = load_cols(ctx, W["b2c"], 16, f"b2{tag}")
        sb1c, sb1k = load_cols(ctx, W["sb1c"], 32, f"sb1{tag}")
        sb2c, sb2k = load_cols(ctx, W["sb2c"], 8, f"sb2{tag}")
        g0b = ctx.sb([128, T], F32); g0k = ctx.fresh(f"g0{tag}")
        g1b = ctx.sb([128, T], F32); g1k = ctx.fresh(f"g1{tag}")
        g1row = ctx.sb([1, T], F32); g1rk = ctx.fresh(f"g1r{tag}")
        gsb = ctx.sb([1, 1024], F32); gsk = ctx.fresh(f"gs{tag}")
        def evac_g(oabs, t0, tw, pap, bk):
            def f(eng, pap=pap, tw=tw):
                return eng.tensor_copy(gsb[:, 0:tw], pap[0:1, :])
            p.op("dve", f, reads={bk}, writes={gsk}, tag=f"evg_{tag}")
            def f1b(eng, pap=pap, tw=tw):
                return eng.tensor_copy(gsb[:, 512:512 + tw], pap[32:33, :])
            p.op("dve", f1b, reads={bk}, writes={gsk}, tag=f"evg1b_{tag}")
            def f2(eng, t0=t0, tw=tw):
                return eng.tensor_tensor(g1row[:, t0:t0 + tw], gsb[:, 512:512 + tw],
                                         gsb[:, 0:tw], ALU.is_gt)
            p.op("dve", f2, reads={gsk}, writes={g1rk}, tag=f"evg2_{tag}")
            bg, bgk = ctx.take_bank()
            def f3(eng, t0=t0, tw=tw, bg=bg):
                return eng.matmul(bg[:, 0:tw], ctx.ones1_f32[:, :],
                                  g1row[:, t0:t0 + tw], start=True, stop=True)
            p.op("pe", f3, reads={g1rk, "ones1"}, writes={bgk}, tag=f"evg3_{tag}")
            def f4(eng, t0=t0, tw=tw, bg=bg):
                return eng.tensor_copy(g1b[:, t0:t0 + tw], bg[:, 0:tw])
            p.op("dve", f4, reads={bgk}, writes={g1k}, tag=f"evg4_{tag}")
            def f5(eng, t0=t0, tw=tw, bg=bg):
                return eng.tensor_scalar(g0b[:, t0:t0 + tw], bg[:, 0:tw], -1.0, 1.0,
                                         ALU.mult, ALU.add)
            p.op("dve", f5, reads={bgk, "consts"}, writes={g0k}, tag=f"evg5_{tag}")
        emit_linear(ctx, D, 64, W["gate_w"],
                    lambda kt, t0, tw: xn[:, kt * T + t0:kt * T + t0 + tw],
                    lambda kt: {xnk}, evac_g, f"gw{tag}", ot_size=64)
        ctx.dump("g1r", g1row[:, :], {g1rk})
        rb1 = ctx.sb([128, 64], BF16); rb1k = ctx.fresh(f"rb1{tag}")
        def frb(eng):
            return eng.activation(rb1[:, :], b1c[:, :], AF.Relu)
        p.op("act", frb, reads={b1k}, writes={rb1k}, tag=f"rb_{tag}")
        negcross = ctx.sb([128, 16], F32); nck = ctx.fresh(f"nc{tag}")
        R = ctx.sb([128, 32 * TH], BF16); Rk = ctx.fresh(f"R{tag}")
        outc = ctx.sb([128, 512], F32); outck = ctx.fresh(f"oc{tag}")
        gb = (g0b, g1b)
        gkk = (g0k, g1k)
        for th in range(2):
            toff = th * TH
            for e in range(2):
                def evac_z(oabs, t0, tw, pap, bk, e=e):
                    ot = oabs // 128
                    def f(eng, ot=ot, t0=t0, tw=tw, pap=pap, e=e):
                        return eng.activation(R[:, ot * TH + t0:ot * TH + t0 + tw],
                                              pap, AF.Relu,
                                              bias=b1c[:, e * 32 + ot:e * 32 + ot + 1])
                    p.op("act", f, reads={bk, b1k}, writes={Rk}, tag=f"evz_{tag}")
                emit_linear(ctx, D, 4096, W[f"w1_{e}"],
                            lambda kt, t0, tw, toff=toff: xn[:, kt * T + toff + t0:
                                                             kt * T + toff + t0 + tw],
                            lambda kt: {xnk}, evac_z, f"w1{e}t{th}{tag}", ntok=TH)
                pre = None
                if th == 0:
                    def pc_evac(oabs, papc, bk, e=e):
                        ot = oabs // 128
                        def f(eng, ot=ot, papc=papc, e=e):
                            return eng.tensor_single_scalar(
                                negcross[:, e * 8 + ot:e * 8 + ot + 1], papc, -1.0,
                                ALU.mult)
                        p.op("dve", f, reads={bk, "consts"}, writes={nck},
                             tag=f"cx_{tag}")
                    pre = (lambda kt, e=e: rb1[:, e * 32 + kt:e * 32 + kt + 1],
                           lambda kt: {rb1k}, 1, pc_evac)
                def evac_s(oabs, t0, tw, pap, bk, e=e, toff=toff):
                    ot = oabs // 128
                    def f(eng, ot=ot, t0=t0, tw=tw, pap=pap, e=e, toff=toff):
                        return eng.scalar_tensor_tensor(
                            outc[:, 0:tw], pap, negcross[:, e * 8 + ot:e * 8 + ot + 1],
                            gb[e][:, toff + t0:toff + t0 + tw], ALU.add, ALU.mult)
                    p.op("dve", f, reads={bk, nck, gkk[e]}, writes={outck},
                         tag=f"evs_{tag}")
                    emit_resid_accum(ctx, outc[:, 0:tw], {outck}, ot, toff + t0, tw,
                                     f"{tag}s{e}")
                emit_linear(ctx, 4096, D, W[f"w2_{e}"],
                            lambda kt, t0, tw: R[:, kt * TH + t0:kt * TH + t0 + tw],
                            lambda kt: {Rk}, evac_s, f"w2{e}t{th}{tag}", ntok=TH,
                            pre_chunk=pre)
            if th == 1:
                ctx.dump("ncx", negcross[:, :], {nck})
            def evac_zs(oabs, t0, tw, pap, bk):
                ot = oabs // 128
                def f(eng, ot=ot, t0=t0, tw=tw, pap=pap):
                    return eng.activation(R[:, ot * TH + t0:ot * TH + t0 + tw], pap,
                                          AF.Relu, bias=sb1c[:, ot:ot + 1])
                p.op("act", f, reads={bk, sb1k}, writes={Rk}, tag=f"evzs_{tag}")
            emit_linear(ctx, D, 4096, W["sw1"],
                        lambda kt, t0, tw, toff=toff: xn[:, kt * T + toff + t0:
                                                         kt * T + toff + t0 + tw],
                        lambda kt: {xnk}, evac_zs, f"sw1t{th}{tag}", ntok=TH)
            if th == 0:
                bt = ctx.sb([128, 8], F32); btk = ctx.fresh(f"bt{tag}")
                def fb1_(eng):
                    return eng.tensor_tensor(bt[:, :], b2c[:, 0:8], b2c[:, 8:16],
                                             ALU.add)
                p.op("dve", fb1_, reads={b2k}, writes={btk}, tag=f"bt1_{tag}")
                def fb2_(eng):
                    return eng.tensor_tensor(bt[:, :], bt[:, :], sb2c[:, :], ALU.add)
                p.op("dve", fb2_, reads={btk, sb2k}, writes={btk}, tag=f"bt2_{tag}")
                def fb3_(eng):
                    return eng.tensor_tensor(bt[:, :], bt[:, :], negcross[:, 0:8],
                                             ALU.subtract)
                p.op("dve", fb3_, reads={btk, nck}, writes={btk}, tag=f"bt3_{tag}")
                def fb4_(eng):
                    return eng.tensor_tensor(bt[:, :], bt[:, :], negcross[:, 8:16],
                                             ALU.subtract)
                p.op("dve", fb4_, reads={btk, nck}, writes={btk}, tag=f"bt4_{tag}")
            def evac_ss(oabs, t0, tw, pap, bk, toff=toff):
                ot = oabs // 128
                def f(eng, ot=ot, t0=t0, tw=tw, pap=pap):
                    return eng.tensor_scalar(outc[:, 0:tw], pap, bt[:, ot:ot + 1],
                                             None, ALU.add)
                p.op("dve", f, reads={bk, btk}, writes={outck}, tag=f"evss_{tag}")
                emit_resid_accum(ctx, outc[:, 0:tw], {outck}, ot, toff + t0, tw,
                                 f"{tag}sh")
            emit_linear(ctx, 4096, D, W["sw2"],
                        lambda kt, t0, tw: R[:, kt * TH + t0:kt * TH + t0 + tw],
                        lambda kt: {Rk}, evac_ss, f"sw2t{th}{tag}", ntok=TH)


def _wdict_l3_ffn(inputs):
    L = "L3_ffn"
    gw = np.asarray(inputs[f"{L}_gate_w"], np.float32)
    gwp = np.zeros((1024, 64), np.float32)
    gwp[:, 0] = gw[0]
    gwp[:, 32] = gw[1]
    b1 = np.asarray(inputs[f"{L}_b1"], np.float32)
    b1c = np.concatenate([prep_cols(b1[0]), prep_cols(b1[1])], 1)
    b2 = np.asarray(inputs[f"{L}_b2"], np.float32)
    b2c = np.concatenate([prep_cols(b2[0]), prep_cols(b2[1])], 1)
    w1 = np.asarray(inputs[f"{L}_w1"], np.float32)
    w2 = np.asarray(inputs[f"{L}_w2"], np.float32)
    return {
        f"{L}_gate_w": prep_wtiled(gwp, 64),
        f"{L}_w1_0": prep_wtiled(np.ascontiguousarray(w1[0])),
        f"{L}_w1_1": prep_wtiled(np.ascontiguousarray(w1[1])),
        f"{L}_b1c": b1c,
        f"{L}_w2_0": prep_wtiled(np.ascontiguousarray(w2[0])),
        f"{L}_w2_1": prep_wtiled(np.ascontiguousarray(w2[1])),
        f"{L}_b2c": b2c,
        f"{L}_sw1": prep_wtiled(np.asarray(inputs[f"{L}_sw1"], np.float32)),
        f"{L}_sb1c": prep_cols(np.asarray(inputs[f"{L}_sb1"], np.float32)),
        f"{L}_sw2": prep_wtiled(np.asarray(inputs[f"{L}_sw2"], np.float32)),
        f"{L}_sb2c": prep_cols(np.asarray(inputs[f"{L}_sb2"], np.float32)),
    }


def _emit_l3_ffn(ctx, aps):
    L = "L3_ffn"
    W = {k: aps[f"{L}_{k}"] for k in ("gate_w", "w1_0", "w1_1", "b1c", "w2_0",
                                      "w2_1", "b2c", "sw1", "sb1c", "sw2", "sb2c")}
    emit_l3_ffn(ctx, W)


BLOCKS["L3_ffn"] = (_wdict_l3_ffn, _emit_l3_ffn)



def emit_l1_attn(ctx, W, tag="l1a"):
    p = ctx.prog
    with ctx.scope():
        xn = ctx.sb([128, NKT * T], BF16)
        xnk = ctx.fresh(f"xn{tag}")
        emit_ln(ctx, PlainView(xn), xnk, f"ln{tag}")
        cos2 = ctx.sb([128, T], BF16); sin2 = ctx.sb([128, T], BF16)
        trig_k = ctx.fresh(f"trig{tag}")
        eye2b = ctx.sb([128, 64], BF16)
        eye128 = ctx.sb([128, 128], F32)
        trimask = ctx.sb([128, 4 * 512], BF16)
        for t_, ap_name in ((eye2b, "c_eye2b"), (eye128, "c_eye128"),
                            (trimask, "c_trimask")):
            def fl(eng, t_=t_, ap_name=ap_name):
                return eng.dma_start(out=t_[:, :], in_=W[ap_name][:, :])
            p.dma("sp", fl, reads=set(), writes={trig_k}, tag=f"ldc_{tag}")
        for t_, ap_name in ((cos2, "c_cos1"), (sin2, "c_sin1")):
            def fl2(eng, t_=t_, ap_name=ap_name):
                return eng.dma_start(out=t_[:, :], in_=W[ap_name][:, :])
            p.dma("pool", fl2, reads=set(), writes={trig_k}, tag=f"ldc2_{tag}")
        q_rot = ctx.sb([128, NKT * T], BF16); qk_ = ctx.fresh(f"q{tag}")
        k_rot = ctx.sb([128, NKT * T], BF16); kk_ = ctx.fresh(f"k{tag}")
        v_tm = ctx.sb([128, 16 * 1040], BF16); vk_ = ctx.fresh(f"v{tag}")
        v_fm = ctx.scratch_bf; vfk = "scrbf"
        scr2 = ctx.sb([128, 512], F32); scr2k = ctx.fresh(f"s2{tag}")
        def fones(eng):
            return eng.memset(
                v_tm[:, :].rearrange("p (tt h c) -> p tt h c", tt=16, h=16)[:, :, :, 64:65],
                1.0)
        p.op("dve", fones, reads=set(), writes={vk_}, tag=f"vo_{tag}")

        def rope_evac(dst, dstk, ot_rel, t0, tw, pap, bk):
            scr = ctx.scratch_f32
            def s1(eng, pap=pap, tw=tw):
                return eng.tensor_single_scalar(scr[0:32, 0:tw], pap[32:64, :],
                                                -1.0, ALU.mult)
            p.op("dve", s1, reads={bk, "consts"}, writes={"scr32"}, tag=f"r1_{tag}")
            def s2(eng, pap=pap, tw=tw):
                return eng.tensor_copy(scr[32:64, 0:tw], pap[0:32, :])
            p.op("dve", s2, reads={bk}, writes={"scr32"}, tag=f"r2_{tag}")
            def s3(eng, pap=pap, tw=tw):
                return eng.tensor_single_scalar(scr[64:96, 0:tw], pap[96:128, :],
                                                -1.0, ALU.mult)
            p.op("dve", s3, reads={bk, "consts"}, writes={"scr32"}, tag=f"r3_{tag}")
            def s4(eng, pap=pap, tw=tw):
                return eng.tensor_copy(scr[96:128, 0:tw], pap[64:96, :])
            p.op("dve", s4, reads={bk}, writes={"scr32"}, tag=f"r4_{tag}")
            def p1(eng, pap=pap, t0=t0, tw=tw):
                return eng.tensor_tensor(scr2[:, 0:tw], pap, cos2[:, t0:t0 + tw],
                                         ALU.mult)
            p.op("dve", p1, reads={bk, trig_k}, writes={scr2k}, tag=f"r5_{tag}")
            def p2(eng, t0=t0, tw=tw, scr=scr):
                return eng.tensor_tensor(scr[:, 0:tw], scr[:, 0:tw],
                                         sin2[:, t0:t0 + tw], ALU.mult)
            p.op("dve", p2, reads={"scr32", trig_k}, writes={"scr32"}, tag=f"r6_{tag}")
            def p3(eng, ot_rel=ot_rel, t0=t0, tw=tw, scr=scr, dst=dst):
                return eng.tensor_tensor(dst[:, ot_rel * T + t0:ot_rel * T + t0 + tw],
                                         scr2[:, 0:tw], scr[:, 0:tw], ALU.add)
            p.op("dve", p3, reads={scr2k, "scr32"}, writes={dstk}, tag=f"r7_{tag}")

        def evac_qkv(oabs, t0, tw, pap, bk):
            ot = oabs // 128
            if ot < 8:
                rope_evac(q_rot, qk_, ot, t0, tw, pap, bk)
            elif ot < 16:
                rope_evac(k_rot, kk_, ot - 8, t0, tw, pap, bk)
            else:
                otv = ot - 16
                def fv(eng, t0=t0, tw=tw, pap=pap):
                    return eng.tensor_copy(v_fm[:, t0:t0 + tw], pap)
                p.op("dve", fv, reads={bk}, writes={vfk}, tag=f"ev_{tag}")
                if t0 + tw >= T:
                    for tkt in range(16):
                        for hh in range(2):
                            h = otv * 2 + hh
                            bb, bbk = ctx.take_bf_bank()
                            tp_ap = bb[:, 0:64]
                            def ft(eng, tkt=tkt, hh=hh, tp_ap=tp_ap):
                                return eng.transpose(
                                    tp_ap, v_fm[hh * 64:hh * 64 + 64,
                                                tkt * 128:tkt * 128 + 128],
                                    eye2b[hh * 64:hh * 64 + 64, :])
                            p.op("pe", ft, reads={vfk, trig_k}, writes={bbk},
                                 tag=f"vt_{tag}")
                            def fc(eng, tkt=tkt, h=h, tp_ap=tp_ap):
                                return eng.tensor_copy(
                                    v_tm[:, tkt * 1040 + h * 65:tkt * 1040 + h * 65 + 64],
                                    tp_ap)
                            p.op("dve", fc, reads={bbk}, writes={vk_}, tag=f"vc_{tag}")
        emit_linear(ctx, D, 3072, W["qkv"],
                    lambda kt, t0, tw: xn[:, kt * T + t0:kt * T + t0 + tw],
                    lambda kt: {xnk}, evac_qkv, f"qkv{tag}")
        mq = ctx.sb([128, 2], F32); mqk = ctx.fresh(f"mq{tag}")
        rowtmp = ctx.sb([1, 128], F32); rk2 = ctx.fresh(f"rt{tag}")
        mrow = ctx.sb([1, 4], F32); mrk = ctx.fresh(f"mr{tag}")
        negC = ctx.sb([128, 1], F32); nCk = ctx.fresh(f"nC{tag}")
        for si, (src_, sk_) in enumerate(((q_rot, qk_), (k_rot, kk_))):
            for ot in range(NKT):
                def fm_(eng, src_=src_, ot=ot, si=si):
                    return eng.reduce_max(ctx.scratch_f32[:, 0:1],
                                          src_[:, ot * T:(ot + 1) * T],
                                          axis=AX.X, apply_absolute_value=True)
                p.op("dve", fm_, reads={sk_}, writes={"scr32"}, tag=f"gm_{tag}")
                if ot == 0:
                    def fc_(eng, si=si):
                        return eng.tensor_copy(mq[:, si:si + 1], ctx.scratch_f32[:, 0:1])
                    p.op("dve", fc_, reads={"scr32"}, writes={mqk}, tag=f"gc_{tag}")
                else:
                    def fx_(eng, si=si):
                        return eng.tensor_tensor(mq[:, si:si + 1], mq[:, si:si + 1],
                                                 ctx.scratch_f32[:, 0:1], ALU.max)
                    p.op("dve", fx_, reads={"scr32", mqk}, writes={mqk}, tag=f"gx_{tag}")
            bt_, btk_ = ctx.take_bank()
            def ftp(eng, si=si, bt_=bt_):
                return eng.transpose(bt_[0:1, 0:128], mq[:, si:si + 1], eye128[:, :])
            p.op("pe", ftp, reads={mqk, trig_k}, writes={btk_}, tag=f"gt_{tag}")
            def fcr(eng, bt_=bt_):
                return eng.tensor_copy(rowtmp[:, :], bt_[0:1, 0:128])
            p.op("dve", fcr, reads={btk_}, writes={rk2}, tag=f"gr_{tag}")
            def frd(eng, si=si):
                return eng.reduce_max(mrow[:, si:si + 1], rowtmp[:, :], axis=AX.X)
            p.op("dve", frd, reads={rk2}, writes={mrk}, tag=f"gd_{tag}")
        def fC(eng):
            return eng.tensor_tensor(mrow[:, 2:3], mrow[:, 0:1], mrow[:, 1:2], ALU.mult)
        p.op("dve", fC, reads={mrk}, writes={mrk}, tag=f"gC_{tag}")
        def fC2(eng):
            return eng.tensor_single_scalar(mrow[:, 2:3], mrow[:, 2:3], -8.0, ALU.mult)
        p.op("dve", fC2, reads={mrk, "consts"}, writes={mrk}, tag=f"gC2_{tag}")
        bC, bCk2 = ctx.take_bank()
        def fbc(eng, bC=bC):
            return eng.matmul(bC[:, 0:1], ctx.ones1_f32[:, :], mrow[:, 2:3],
                              start=True, stop=True)
        p.op("pe", fbc, reads={mrk, "ones1"}, writes={bCk2}, tag=f"gb_{tag}")
        def fbc2(eng, bC=bC):
            return eng.tensor_copy(negC[:, :], bC[:, 0:1])
        p.op("dve", fbc2, reads={bCk2}, writes={nCk}, tag=f"gb2_{tag}")
        ao, aok = xn, xnk
        p_sb = ctx.sb([128, 16 * 512], BF16); psk = ctx.fresh(f"ps{tag}")
        rrow = ctx.sb([1, 512], F32); rrk = ctx.fresh(f"rr{tag}")
        for h in range(16):
            hb = (h % 2) * 64
            hT = (h // 2) * T
            for tc in range(4):
                ntk = 4 * (tc + 1)
                tq0 = tc * 512
                for tkt in range(ntk):
                    psc, bk = ctx.take_bank()
                    def fs(eng, h=h, hb=hb, hT=hT, tkt=tkt, tq0=tq0, psc=psc):
                        return eng.matmul(
                            psc[:, 0:512],
                            k_rot[hb:hb + 64, hT + tkt * 128:hT + tkt * 128 + 128],
                            q_rot[hb:hb + 64, hT + tq0:hT + tq0 + 512],
                            start=True, stop=True)
                    p.op("pe", fs, reads={qk_, kk_}, writes={bk}, tag=f"sc_{tag}")
                    def fe(eng, tkt=tkt, psc=psc):
                        return eng.activation(p_sb[:, tkt * 512:(tkt + 1) * 512],
                                              psc[:, 0:512], AF.Exp, scale=0.125,
                                              bias=negC[:, 0:1])
                    p.op("act", fe, reads={bk, nCk}, writes={psk}, tag=f"ex_{tag}")
                for m in range(4):
                    kt = 4 * tc + m
                    def fmk(eng, kt=kt, m=m):
                        sl = p_sb[:, kt * 512:(kt + 1) * 512]
                        return eng.tensor_tensor(sl, sl,
                                                 trimask[:, m * 512:(m + 1) * 512],
                                                 ALU.mult)
                    p.op("dve", fmk, reads={psk, trig_k}, writes={psk}, tag=f"mk_{tag}")
                ppv, bkpv = ctx.take_bank()
                for tkt in range(ntk):
                    def fpv(eng, h=h, tkt=tkt, ppv=ppv, st=(tkt == 0),
                            sp=(tkt == ntk - 1)):
                        return eng.matmul(
                            ppv[0:65, 0:512],
                            v_tm[:, tkt * 1040 + h * 65:tkt * 1040 + h * 65 + 65],
                            p_sb[:, tkt * 512:(tkt + 1) * 512],
                            start=st, stop=sp)
                    p.op("pe", fpv, reads={vk_, psk}, writes={bkpv}, tag=f"pv_{tag}")
                def fr(eng, ppv=ppv):
                    return eng.reciprocal(rrow[:, :], ppv[64:65, 0:512])
                p.op("dve", fr, reads={bkpv}, writes={rrk}, tag=f"rc_{tag}")
                pbc, bkbc = ctx.take_bank()
                def fb(eng, pbc=pbc):
                    return eng.matmul(pbc[0:64, 0:512], ctx.ones1_f32[:, 0:64],
                                      rrow[:, :], start=True, stop=True)
                p.op("pe", fb, reads={rrk, "ones1"}, writes={bkbc}, tag=f"bd_{tag}")
                def fcp(eng, ppv=ppv):
                    return eng.tensor_copy(ctx.scratch_f32[0:64, 0:512],
                                           ppv[0:64, 0:512])
                p.op("dve", fcp, reads={bkpv}, writes={"scr32"}, tag=f"cp_{tag}")
                def fdv(eng, hb=hb, hT=hT, tq0=tq0, pbc=pbc):
                    return eng.tensor_tensor(
                        ao[:, :].rearrange("p f -> p f")[hb:hb + 64,
                                                         hT + tq0:hT + tq0 + 512],
                        ctx.scratch_f32[0:64, 0:512], pbc[0:64, 0:512], ALU.mult)
                p.op("dve", fdv, reads={"scr32", bkbc}, writes={aok}, tag=f"dv_{tag}")
        outc = ctx.sb([128, 512], F32); outck = ctx.fresh(f"oc{tag}")
        def evac_o(oabs, t0, tw, pap, bk):
            ot = oabs // 128
            def f(eng, tw=tw, pap=pap):
                return eng.tensor_copy(outc[:, 0:tw], pap)
            p.op("dve", f, reads={bk}, writes={outck}, tag=f"evo_{tag}")
            emit_resid_accum(ctx, outc[:, 0:tw], {outck}, ot, t0, tw, tag)
        emit_linear(ctx, D, D, W["wo"],
                    lambda kt, t0, tw: ao[:, kt * T + t0:kt * T + t0 + tw],
                    lambda kt: {aok}, evac_o, f"wo{tag}")


def _wdict_l1_attn(inputs):
    L = "L1_attn"
    return {
        f"{L}_qkv": prep_wtiled(_t(inputs[f"{L}_qkv"])),
        f"{L}_wo": prep_wtiled(_t(inputs[f"{L}_wo"])),
    }


def _emit_l1_attn(ctx, aps):
    L = "L1_attn"
    W = {"qkv": aps[f"{L}_qkv"], "wo": aps[f"{L}_wo"],
         "c_cos1": aps["c_cos1"], "c_sin1": aps["c_sin1"],
         "c_eye2b": aps["c_eye2b"], "c_eye128": aps["c_eye128"],
         "c_trimask": aps["c_trimask"]}
    emit_l1_attn(ctx, W)


BLOCKS["L1_attn"] = (_wdict_l1_attn, _emit_l1_attn)



def emit_l0_attn(ctx, W, tag="l0a"):
    global L0_TRUNC
    """Mamba selective-scan mixer. Sequence: xn=LN(x) (echoed to x); xin=LN(xn)*g+b;
    xz = xin@W_inT (u otiles 0-15, z 16-31); u_c = silu(conv4(u)+cb); x_dbl parts
    B,C (bc) and dt_low; dt = softplus(dt_low@W_dtT + b_dt); scan per (itile, n);
    y = (scan + u_c*D)*silu(z); out = y@W_outT; x += out (+xn echo)."""
    p = ctx.prog
    TP3 = T + 3
    with ctx.scope():
        u_pad = ctx.sb([128, 16 * TP3], BF16); upk = ctx.fresh(f"up{tag}")
        sz_hbm = ctx.nc.dram_tensor(ctx.fresh("l0sz"), [2048, T], BF16,
                                    kind="Internal").ap()
        szk = ctx.fresh(f"sz{tag}")
        g_cols, gk1 = load_cols(ctx, W["ln_g"], NKT, f"g{tag}")
        b_cols, _ = load_cols(ctx, W["ln_b"], NKT, f"b{tag}", key=gk1)
        cw, cwk = load_cols(ctx, W["conv_w"], 64, f"cw{tag}")
        cb, cbk = load_cols(ctx, W["conv_b"], 16, f"cb{tag}")
        bdt, bdtk = load_cols(ctx, W["b_dt"], 16, f"bdt{tag}")
        Dc, Dck = load_cols(ctx, W["D"], 16, f"D{tag}")
        Alog, Alk = load_cols(ctx, W["A_log"], 256, f"Al{tag}")
        def fA1(eng):
            return eng.activation(Alog[:, :], Alog[:, :], AF.Exp)
        p.op("act", fA1, reads={Alk}, writes={Alk}, tag=f"A1_{tag}")
        def fA2(eng):
            return eng.tensor_single_scalar(Alog[:, :], Alog[:, :], -1.0, ALU.mult)
        p.op("dve", fA2, reads={Alk, "consts"}, writes={Alk}, tag=f"A2_{tag}")
        def fz(eng):
            return eng.memset(
                u_pad[:, :].rearrange("p (it t) -> p it t", it=16)[:, :, 0:3], 0.0)
        p.op("dve", fz, reads=set(), writes={upk}, tag=f"z_{tag}")
        _ph1 = ExitStack()
        ctx._scopes.append(_ph1)
        xn_o = ctx.sb([128, NKT * T], BF16)
        xnok = ctx.fresh(f"xno{tag}")
        xin = ctx.sb([128, NKT * T], BF16)
        xink = ctx.fresh(f"xi{tag}")
        ctx._scopes.pop()
        emit_ln(ctx, PlainView(xn_o), xnok, f"lno{tag}", resid_echo=True)
        emit_ln(ctx, PlainView(xin), xink, f"lni{tag}",
                src_view=PlainView(xn_o), src_key=xnok,
                gain_bias=(g_cols, b_cols), gb_key=gk1)
        if L0_TRUNC <= 1:
            _ph1.close(); p.barrier(); return
        def evac_win(oabs, t0, tw, pap, bk):
            ot = oabs // 128
            if ot < 16:
                def f(eng, ot=ot, t0=t0, tw=tw, pap=pap):
                    return eng.tensor_copy(
                        u_pad[:, ot * TP3 + 3 + t0:ot * TP3 + 3 + t0 + tw], pap)
                p.op("dve", f, reads={bk}, writes={upk}, tag=f"evu_{tag}")
            else:
                otz = ot - 16
                def f(eng, tw=tw, pap=pap):
                    return eng.activation(ctx.scratch_bf[:, 0:tw], pap, AF.Silu)
                p.op("act", f, reads={bk}, writes={"scrbf"}, tag=f"evz_{tag}")
                def f2(eng, otz=otz, t0=t0, tw=tw):
                    return eng.dma_start(
                        out=sz_hbm.rearrange("(it p) t -> p it t", p=128)[:, otz,
                                                                          t0:t0 + tw],
                        in_=ctx.scratch_bf[:, 0:tw])
                p.dma("pool", f2, reads={"scrbf"}, writes={szk}, tag=f"szs_{tag}")
        emit_linear(ctx, D, 4096, W["W_in"],
                    lambda kt, t0, tw: xin[:, kt * T + t0:kt * T + t0 + tw],
                    lambda kt: {xink}, evac_win, f"wi{tag}")
        _ph1.close()
        p.barrier()
        if L0_TRUNC <= 2:
            return
        for it in range(16):
            base = it * TP3
            for c in (3, 2, 1, 0):
                t0 = c * 512
                acc = ctx.scratch_f32
                def c0(eng, base=base, t0=t0, it=it):
                    return eng.tensor_scalar(acc[:, 0:512],
                                             u_pad[:, base + t0:base + t0 + 512],
                                             cw[:, it * 4:it * 4 + 1], None, ALU.mult)
                p.op("dve", c0, reads={upk, cwk}, writes={"scr32"}, tag=f"c0_{tag}")
                for j in (1, 2, 3):
                    def cj(eng, base=base, t0=t0, it=it, j=j):
                        return eng.scalar_tensor_tensor(
                            acc[:, 0:512], u_pad[:, base + t0 + j:base + t0 + j + 512],
                            cw[:, it * 4 + j:it * 4 + j + 1], acc[:, 0:512],
                            ALU.mult, ALU.add)
                    p.op("dve", cj, reads={upk, cwk, "scr32"}, writes={"scr32"},
                         tag=f"c{j}_{tag}")
                def cs(eng, base=base, t0=t0, it=it):
                    return eng.activation(u_pad[:, base + 3 + t0:base + 3 + t0 + 512],
                                          acc[:, 0:512], AF.Silu,
                                          bias=cb[:, it:it + 1])
                p.op("act", cs, reads={"scr32", cbk}, writes={upk}, tag=f"cs_{tag}")
        if L0_TRUNC <= 3:
            return
        bc_sb = ctx.sb([32, T], F32); bck = ctx.fresh(f"bc{tag}")
        def evac_bc(oabs, t0, tw, pap, bk):
            def f(eng, t0=t0, tw=tw, pap=pap):
                return eng.tensor_copy(bc_sb[:, t0:t0 + tw], pap)
            p.op("dve", f, reads={bk}, writes={bck}, tag=f"evbc_{tag}")
        emit_linear(ctx, 2048, 32, W["W_xp_bc"],
                    lambda kt, t0, tw: u_pad[:, kt * TP3 + 3 + t0:kt * TP3 + 3 + t0 + tw],
                    lambda kt: {upk}, evac_bc, f"wbc{tag}", ot_size=32)
        dtl = ctx.sb([128, T], BF16); dtlk = ctx.fresh(f"dtl{tag}")
        def fzd(eng):
            return eng.memset(dtl[64:128, :], 0.0)
        p.op("dve", fzd, reads=set(), writes={dtlk}, tag=f"zd_{tag}")
        def evac_dtl(oabs, t0, tw, pap, bk):
            def f(eng, t0=t0, tw=tw, pap=pap):
                return eng.tensor_copy(dtl[0:64, t0:t0 + tw], pap)
            p.op("dve", f, reads={bk}, writes={dtlk}, tag=f"evdl_{tag}")
        emit_linear(ctx, 2048, 64, W["W_xp_dt"],
                    lambda kt, t0, tw: u_pad[:, kt * TP3 + 3 + t0:kt * TP3 + 3 + t0 + tw],
                    lambda kt: {upk}, evac_dtl, f"wdl{tag}", ot_size=64)
        if L0_TRUNC <= 4:
            return
        dt_sb = ctx.sb([128, T], F32); dtk = ctx.fresh(f"dt{tag}")
        dtu = ctx.sb([128, T], BF16); dtuk = ctx.fresh(f"dtu{tag}")
        sp_a = ctx.sb([128, 512], F32); spk = ctx.fresh(f"sp{tag}")
        dA = ctx.sb([128, T], F32); dAk = ctx.fresh(f"dA{tag}")
        dBu = ctx.sb([128, 512], F32); dBk = ctx.fresh(f"dB{tag}")
        t1 = ctx.sb([128, 512], F32); t1k = ctx.fresh(f"t1{tag}")
        y_acc = ctx.sb([128, T], F32); yak = ctx.fresh(f"ya{tag}")
        h_buf = ctx.sb([128, T + 1], F32); hbk = ctx.fresh(f"hb{tag}")
        brow = ctx.sb([1, T], F32); brk = ctx.fresh(f"br{tag}")
        crow = ctx.sb([1, T], F32); crk = ctx.fresh(f"cr{tag}")
        szt = ctx.sb([128, T], BF16); sztk = ctx.fresh(f"szt{tag}")
        for it in range(2 if (L0_TRUNC <= 5 or 50 <= L0_TRUNC < 99) else 16):
            def evac_dt(oabs, t0, tw, pap, bk, it=it):
                def q1(eng, pap=pap, tw=tw, it=it):
                    return eng.tensor_scalar(ctx.scratch_f32[:, 0:tw], pap,
                                             bdt[:, it:it + 1], 0.0, ALU.add, ALU.max)
                p.op("dve", q1, reads={bk, bdtk, "consts"}, writes={"scr32"},
                     tag=f"sp1_{tag}")
                def q2(eng, pap=pap, tw=tw):
                    return eng.scalar_tensor_tensor(sp_a[:, 0:tw],
                                                    ctx.scratch_f32[:, 0:tw], 2.0,
                                                    pap, ALU.mult, ALU.subtract)
                p.op("dve", q2, reads={bk, "scr32", "consts"}, writes={spk},
                     tag=f"sp2_{tag}")
                def f2(eng, tw=tw, it=it):
                    return eng.activation(sp_a[:, 0:tw], sp_a[:, 0:tw], AF.Exp,
                                          scale=-1.0, bias=bdt[:, it:it + 1])
                p.op("act", f2, reads={spk, bdtk}, writes={spk}, tag=f"sp3_{tag}")
                def f3(eng, tw=tw):
                    return eng.activation(sp_a[:, 0:tw], sp_a[:, 0:tw], AF.Ln, bias=1.0)
                p.op("act", f3, reads={spk, "consts"}, writes={spk}, tag=f"sp4_{tag}")
                def f5(eng, t0=t0, tw=tw):
                    return eng.tensor_tensor(dt_sb[:, t0:t0 + tw],
                                             ctx.scratch_f32[:, 0:tw], sp_a[:, 0:tw],
                                             ALU.add)
                p.op("dve", f5, reads={"scr32", spk}, writes={dtk}, tag=f"sp5_{tag}")
            def evac_dt_copy(oabs, t0, tw, pap, bk, it=it):
                def f(eng, t0=t0, tw=tw, pap=pap):
                    return eng.tensor_copy(dt_sb[:, t0:t0 + tw], pap)
                p.op("dve", f, reads={bk}, writes={dtk}, tag=f"spc_{tag}")
            emit_linear(ctx, 128, 128, W["W_dt"],
                        lambda kt, t0, tw: dtl[:, t0:t0 + tw],
                        lambda kt: {dtlk},
                        evac_dt_copy if L0_TRUNC == 54 else evac_dt,
                        f"wdt{tag}{it}", o0=it * 128, ot_size=128)
            if L0_TRUNC in (50, 54, 55):
                continue
            for c in range(4):
                t0 = c * 512
                def fdtu(eng, t0=t0, it=it):
                    return eng.tensor_tensor(
                        dtu[:, t0:t0 + 512], dt_sb[:, t0:t0 + 512],
                        u_pad[:, it * TP3 + 3 + t0:it * TP3 + 3 + t0 + 512], ALU.mult)
                p.op("dve", fdtu, reads={dtk, upk}, writes={dtuk}, tag=f"du_{tag}")
            def fzy(eng):
                return eng.memset(y_acc[:, :], 0.0)
            p.op("dve", fzy, reads=set(), writes={yak}, tag=f"zy_{tag}")
            if L0_TRUNC == 51:
                continue
            for n in range(16 if L0_TRUNC != 52 else 1):
                def fbr_(eng, n=n):
                    return eng.dma_start(out=brow[:, :], in_=bc_sb[n:n + 1, :])
                p.dma("sp", fbr_, reads={bck}, writes={brk}, tag=f"xb_{tag}")
                def fcr_(eng, n=n):
                    return eng.dma_start(out=crow[:, :], in_=bc_sb[16 + n:17 + n, :])
                p.dma("sp", fcr_, reads={bck}, writes={crk}, tag=f"xc_{tag}")
                def fda(eng, it=it, n=n):
                    return eng.activation(dA[:, :], dt_sb[:, :], AF.Exp,
                                          scale=Alog[:, it * 16 + n:it * 16 + n + 1])
                p.op("act", fda, reads={dtk, Alk}, writes={dAk}, tag=f"dA_{tag}")
                def fzh(eng):
                    return eng.memset(h_buf[:, 0:1], 0.0)
                p.op("dve", fzh, reads=set(), writes={hbk}, tag=f"zh_{tag}")
                for c in range(4):
                    t0 = c * 512
                    bB, bBk = ctx.take_bank()
                    bC, bCk = ctx.take_bank()
                    def fb1(eng, bB=bB, t0=t0):
                        return eng.matmul(bB[:, 0:512], ctx.ones1_f32[:, :],
                                          brow[:, t0:t0 + 512], start=True, stop=True)
                    p.op("pe", fb1, reads={brk, "ones1"}, writes={bBk}, tag=f"bB_{tag}")
                    def fb2(eng, bC=bC, t0=t0):
                        return eng.matmul(bC[:, 0:512], ctx.ones1_f32[:, :],
                                          crow[:, t0:t0 + 512], start=True, stop=True)
                    p.op("pe", fb2, reads={crk, "ones1"}, writes={bCk}, tag=f"bC_{tag}")
                    def fdb(eng, t0=t0, bB=bB):
                        return eng.tensor_tensor(dBu[:, :], dtu[:, t0:t0 + 512],
                                                 bB[:, 0:512], ALU.mult)
                    p.op("dve", fdb, reads={dtuk, bBk}, writes={dBk}, tag=f"dBu_{tag}")
                    def fsc(eng, t0=t0):
                        return eng.tensor_tensor_scan(h_buf[:, 1 + t0:513 + t0],
                                                      dA[:, t0:t0 + 512], dBu[:, :],
                                                      h_buf[:, t0:t0 + 1],
                                                      ALU.mult, ALU.add)
                    p.op("dve", fsc, reads={dAk, dBk, hbk}, writes={hbk},
                         tag=f"sc_{tag}")
                    def fcm(eng, bC=bC, t0=t0):
                        return eng.tensor_tensor(t1[:, :], h_buf[:, 1 + t0:513 + t0],
                                                 bC[:, 0:512], ALU.mult)
                    p.op("dve", fcm, reads={hbk, bCk}, writes={t1k}, tag=f"cm_{tag}")
                    def fya(eng, t0=t0):
                        return eng.tensor_tensor(y_acc[:, t0:t0 + 512],
                                                 y_acc[:, t0:t0 + 512], t1[:, :],
                                                 ALU.add)
                    p.op("dve", fya, reads={yak, t1k}, writes={yak}, tag=f"ya_{tag}")
            if L0_TRUNC == 53:
                continue
            def fszl(eng, it=it):
                return eng.dma_start(
                    out=szt[:, :],
                    in_=sz_hbm.rearrange("(it p) t -> p it t", p=128)[:, it, :])
            p.dma("sp", fszl, reads={szk}, writes={sztk}, tag=f"szl_{tag}")
            for c in range(4):
                t0 = c * 512
                def fg1(eng, t0=t0, it=it):
                    return eng.scalar_tensor_tensor(
                        t1[:, :], u_pad[:, it * TP3 + 3 + t0:it * TP3 + 3 + t0 + 512],
                        Dc[:, it:it + 1], y_acc[:, t0:t0 + 512], ALU.mult, ALU.add)
                p.op("dve", fg1, reads={upk, Dck, yak}, writes={t1k}, tag=f"g1_{tag}")
                def fg2(eng, t0=t0, it=it):
                    return eng.tensor_tensor(
                        u_pad[:, it * TP3 + 3 + t0:it * TP3 + 3 + t0 + 512],
                        t1[:, :], szt[:, t0:t0 + 512], ALU.mult)
                p.op("dve", fg2, reads={t1k, sztk}, writes={upk}, tag=f"g2_{tag}")
        if L0_TRUNC <= 6:
            return
        outc = ctx.sb([128, 512], F32); outck = ctx.fresh(f"oc{tag}")
        def evac_o(oabs, t0, tw, pap, bk):
            ot = oabs // 128
            def f(eng, tw=tw, pap=pap):
                return eng.tensor_copy(outc[:, 0:tw], pap)
            p.op("dve", f, reads={bk}, writes={outck}, tag=f"evo_{tag}")
            emit_resid_accum(ctx, outc[:, 0:tw], {outck}, ot, t0, tw, tag)
        emit_linear(ctx, 2048, D, W["W_out"],
                    lambda kt, t0, tw: u_pad[:, kt * TP3 + 3 + t0:kt * TP3 + 3 + t0 + tw],
                    lambda kt: {upk}, evac_o, f"wo{tag}")


def _wdict_l0_attn(inputs):
    L = "L0_attn"
    WxpT = _t(inputs[f"{L}_W_xp"])
    WdtT = _t(inputs[f"{L}_W_dt"])
    WdtT_pad = np.zeros((128, 2048), np.float32)
    WdtT_pad[0:64] = WdtT
    cw = np.asarray(inputs[f"{L}_conv_w"], np.float32)
    cw_cols = cw.reshape(16, 128, 4).transpose(1, 0, 2).reshape(128, 64)
    return {
        f"{L}_W_in": prep_wtiled(_t(inputs[f"{L}_W_in"])),
        f"{L}_W_xp_bc": prep_wtiled(np.ascontiguousarray(WxpT[:, 64:96]), 32),
        f"{L}_W_xp_dt": prep_wtiled(np.ascontiguousarray(WxpT[:, 0:64]), 64),
        f"{L}_W_dt": prep_wtiled(WdtT_pad, 128),
        f"{L}_W_out": prep_wtiled(_t(inputs[f"{L}_W_out"])),
        f"{L}_conv_w": np.ascontiguousarray(cw_cols),
        f"{L}_conv_b": prep_cols(np.asarray(inputs[f"{L}_conv_b"], np.float32)),
        f"{L}_b_dt": prep_cols(np.asarray(inputs[f"{L}_b_dt"], np.float32)),
        f"{L}_A_log": np.ascontiguousarray(
            np.asarray(inputs[f"{L}_A_log"], np.float32).reshape(16, 128, 16)
            .transpose(1, 0, 2).reshape(128, 256)),
        f"{L}_D": prep_cols(np.asarray(inputs[f"{L}_D"], np.float32)),
        f"{L}_ln_g": prep_cols(np.asarray(inputs[f"{L}_ln_g"], np.float32)),
        f"{L}_ln_b": prep_cols(np.asarray(inputs[f"{L}_ln_b"], np.float32)),
    }


def _emit_l0_attn(ctx, aps):
    L = "L0_attn"
    W = {k: aps[f"{L}_{k}"] for k in ("W_in", "W_xp_bc", "W_xp_dt", "W_dt", "W_out",
                                      "conv_w", "conv_b", "b_dt", "A_log", "D",
                                      "ln_g", "ln_b")}
    emit_l0_attn(ctx, W)


BLOCKS["L0_attn"] = (_wdict_l0_attn, _emit_l0_attn)



def emit_l0_ffn(ctx, W, tag="l0f"):
    p = ctx.prog
    NQ = 256
    with ctx.scope():
        xn = ctx.sb([128, NKT * T], BF16)
        xnk = ctx.fresh(f"xn{tag}")
        mrow, mk, rrow, rk = emit_ln(ctx, PlainView(xn), xnk, f"ln{tag}")
        rw, rwk = load_cols(ctx, W["router_w"], NKT, f"rw{tag}")
        ln1c, l1k = load_cols(ctx, W["ln1"], NKT, f"l1{tag}")
        ln2c, l2k = load_cols(ctx, W["ln2"], NKT, f"l2{tag}")
        rb = ctx.sb([1, 1], F32); rbk = ctx.fresh(f"rb{tag}")
        def flrb(eng):
            return eng.dma_start(out=rb[:, :], in_=W["router_b"][:, :])
        p.dma("sp", flrb, reads=set(), writes={rbk}, tag=f"lrb_{tag}")
        qiota = ctx.sb([128, NQ], F32)
        e255 = ctx.sb([1, 128], F32)
        n1e9 = ctx.sb([1, NQ], F32)
        km = ctx.sb([128, NQ], BF16)
        cnk = ctx.fresh(f"cn{tag}")
        for t_, nm in ((qiota, "c_qiota1"), (e255, "c_e255"), (n1e9, "c_neg1e9"),
                       (km, "c_km")):
            def flc(eng, t_=t_, nm=nm):
                return eng.dma_start(out=t_[:, :], in_=W[nm][:, :])
            p.dma("sp", flc, reads=set(), writes={cnk}, tag=f"lc_{tag}")
        eye128 = ctx.sb([128, 128], F32); eyek = ctx.fresh(f"eye{tag}")
        def fley(eng):
            return eng.dma_start(out=eye128[:, :], in_=W["c_eye128"][:, :])
        p.dma("sp", fley, reads=set(), writes={eyek}, tag=f"ley_{tag}")
        eye2b = ctx.sb([128, 64], BF16); ey2k = ctx.fresh(f"ey2{tag}")
        def fley2(eng):
            return eng.dma_start(out=eye2b[:, :], in_=W["c_eye2b"][:, :])
        p.dma("sp", fley2, reads=set(), writes={ey2k}, tag=f"ley2_{tag}")
        eye128b = ctx.sb([128, 128], BF16); ey128bk = ctx.fresh(f"ey128b{tag}")
        def fley3(eng):
            return eng.dma_start(out=eye128b[:, :], in_=W["c_eye128b"][:, :])
        p.dma("sp", fley3, reads=set(), writes={ey128bk}, tag=f"ley3_{tag}")
        rank_tm = ctx.sb([128, 16], F32); rtmk = ctx.fresh(f"rtm{tag}")
        mask_tm = ctx.sb([128, 16], F32); mtmk = ctx.fresh(f"mtm{tag}")
        s128 = ctx.sb([128, 16], F32); s128k = ctx.fresh(f"s128{tag}")
        Fb = ctx.sb([128, T], F32); Fbk = ctx.fresh(f"Fb{tag}")
        _sa = ExitStack()
        ctx._scopes.append(_sa)
        srow = ctx.sb([1, T], F32); srk = ctx.fresh(f"sr{tag}")
        c0 = ctx.sb([1, 1], F32); c0k = ctx.fresh(f"c0{tag}")
        bc0, bc0k = ctx.take_bank()
        for kt in range(NKT):
            def f0(eng, kt=kt, bc0=bc0):
                return eng.matmul(bc0[0:1, 0:1], ctx.onesc_f32[:, :],
                                  rw[:, kt:kt + 1], start=(kt == 0),
                                  stop=(kt == NKT - 1))
            p.op("pe", f0, reads={rwk, "ones1"}, writes={bc0k}, tag=f"c0_{tag}")
        def f0c(eng, bc0=bc0):
            return eng.tensor_copy(c0[:, :], bc0[0:1, 0:1])
        p.op("dve", f0c, reads={bc0k}, writes={c0k}, tag=f"c0c_{tag}")
        _rs = ExitStack()
        ctx._scopes.append(_rs)
        xc2 = ctx.sb([128, NKT * 512], F32)
        ctx._scopes.pop()
        xck2 = ctx.fresh(f"xc{tag}")
        xrh = ctx.x_hbm.rearrange("(kt p) t -> p kt t", p=128)
        for c in range(4):
            t0 = c * 512
            def fldx(eng, t0=t0):
                return eng.dma_start(
                    out=xc2[:, :].rearrange("p (kt t) -> p kt t", kt=NKT),
                    in_=xrh[:, :, t0:t0 + 512])
            p.dma("sp", fldx, reads=xkeys(t0, 512), writes={xck2}, tag=f"ldx_{tag}")
            bs, bsk = ctx.take_bank()
            for kt in range(NKT):
                def f1(eng, kt=kt, bs=bs):
                    return eng.matmul(bs[0:1, 0:512], rw[:, kt:kt + 1],
                                      xc2[:, kt * 512:(kt + 1) * 512],
                                      start=(kt == 0), stop=(kt == NKT - 1))
                p.op("pe", f1, reads={rwk, xck2}, writes={bsk}, tag=f"s_{tag}")
            def f2(eng, t0=t0):
                return eng.tensor_scalar(srow[:, t0:t0 + 512], mrow[:, t0:t0 + 512],
                                         c0[0:1, 0:1], None, ALU.mult)
            p.op("dve", f2, reads={mk, c0k}, writes={srk}, tag=f"s2_{tag}")
            def f3(eng, t0=t0, bs=bs):
                return eng.tensor_tensor(srow[:, t0:t0 + 512], bs[0:1, 0:512],
                                         srow[:, t0:t0 + 512], ALU.subtract)
            p.op("dve", f3, reads={srk, bsk}, writes={srk}, tag=f"s3_{tag}")
            def f4(eng, t0=t0):
                return eng.tensor_tensor(srow[:, t0:t0 + 512], srow[:, t0:t0 + 512],
                                         rrow[:, t0:t0 + 512], ALU.mult)
            p.op("dve", f4, reads={srk, rk}, writes={srk}, tag=f"s4_{tag}")
            def f5(eng, t0=t0):
                return eng.tensor_scalar(srow[:, t0:t0 + 512], srow[:, t0:t0 + 512],
                                         rb[0:1, 0:1], None, ALU.add)
            p.op("dve", f5, reads={srk, rbk}, writes={srk}, tag=f"s5_{tag}")
        _rs.close()
        p.barrier()
        for tt in range(16):
            bt_, btk_ = ctx.take_bank()
            def fts(eng, tt=tt, bt_=bt_):
                return eng.transpose(bt_[:, 0:1], srow[:, tt * 128:(tt + 1) * 128],
                                     eye128[0:1, 0:1])
            p.op("pe", fts, reads={srk, eyek}, writes={btk_}, tag=f"ts_{tag}")
            def fcs(eng, tt=tt, bt_=bt_):
                return eng.tensor_copy(s128[:, tt:tt + 1], bt_[:, 0:1])
            p.op("dve", fcs, reads={btk_}, writes={s128k}, tag=f"cs128_{tag}")
        bis = ctx.sb([1, 8], F32); bisk = ctx.fresh(f"bis{tag}")
        bisi = ctx.sb([1, 1], I32); bisik = ctx.fresh(f"bisi{tag}")
        def fb0(eng):
            return eng.memset(bis[:, 0:1], -1e4)
        p.op("dve", fb0, reads=set(), writes={bisk}, tag=f"b0_{tag}")
        def fb1(eng):
            return eng.memset(bis[:, 1:2], 1e4)
        p.op("dve", fb1, reads=set(), writes={bisk}, tag=f"b1_{tag}")
        cntc = ctx.sb([128, 1], F32); cck = ctx.fresh(f"cc{tag}")
        cnt2 = ctx.sb([128, 1], F32); cc2k = ctx.fresh(f"cc2x{tag}")
        for it_ in range(48):
            def fm_(eng):
                return eng.tensor_tensor(bis[:, 2:3], bis[:, 0:1], bis[:, 1:2], ALU.add)
            p.op("dve", fm_, reads={bisk}, writes={bisk}, tag=f"bm_{tag}")
            def fm2(eng):
                return eng.tensor_single_scalar(bis[:, 2:3], bis[:, 2:3], 0.5, ALU.mult)
            p.op("dve", fm2, reads={bisk, "consts"}, writes={bisk}, tag=f"bm2_{tag}")
            bmid, bmk_ = ctx.take_bank()
            def fbc_(eng, bmid=bmid):
                return eng.matmul(bmid[:, 0:1], ctx.ones1_f32[:, :], bis[:, 2:3],
                                  start=True, stop=True)
            p.op("pe", fbc_, reads={bisk, "ones1"}, writes={bmk_}, tag=f"bb_{tag}")
            def fcp_(eng, bmid=bmid):
                return eng.tensor_copy(cntc[:, 0:1], bmid[:, 0:1])
            p.op("dve", fcp_, reads={bmk_}, writes={cck}, tag=f"bc_{tag}")
            def fgt(eng):
                return eng.tensor_scalar(ctx.scratch_f32[:, 0:16], s128[:, :],
                                         cntc[:, 0:1], 0.0, ALU.is_gt, ALU.add,
                                         accum_out=cnt2[:, 0:1])
            p.op("dve", fgt, reads={s128k, cck}, writes={"scr32", cc2k}, tag=f"bg_{tag}")
            bsum, bsk2 = ctx.take_bank()
            def fsm(eng, bsum=bsum):
                return eng.matmul(bsum[0:1, 0:1], cnt2[:, 0:1], ctx.onesc_f32[:, :],
                                  start=True, stop=True)
            p.op("pe", fsm, reads={cc2k, "ones1"}, writes={bsk2}, tag=f"bs_{tag}")
            def fcnt(eng, bsum=bsum):
                return eng.tensor_copy(bis[:, 3:4], bsum[0:1, 0:1])
            p.op("dve", fcnt, reads={bsk2}, writes={bisk}, tag=f"bn_{tag}")
            def fcond(eng):
                return eng.tensor_single_scalar(bisi[:, :], bis[:, 3:4], 255.5,
                                                ALU.is_gt)
            p.op("dve", fcond, reads={bisk, "consts"}, writes={bisik}, tag=f"bo_{tag}")
            def fsel1(eng):
                return eng.select(bis[:, 0:1], bisi[:, :], bis[:, 2:3], bis[:, 0:1])
            p.op("dve", fsel1, reads={bisik, bisk}, writes={bisk}, tag=f"bp_{tag}")
            def fsel2(eng):
                return eng.select(bis[:, 1:2], bisi[:, :], bis[:, 1:2], bis[:, 2:3])
            p.op("dve", fsel2, reads={bisik, bisk}, writes={bisk}, tag=f"bq_{tag}")
        maskr = ctx.sb([1, T], F32); mrk2 = ctx.fresh(f"msk{tag}")
        def fmsk(eng):
            return eng.tensor_scalar(maskr[:, :], srow[:, :], bis[0:1, 1:2], None,
                                     ALU.is_gt)
        p.op("dve", fmsk, reads={srk, bisk}, writes={mrk2}, tag=f"msk_{tag}")
        onesT = ctx.sb([1, T], F32); oTk = ctx.fresh(f"oT{tag}")
        def foT(eng):
            return eng.memset(onesT[:, :], 1.0)
        p.op("dve", foT, reads=set(), writes={oTk}, tag=f"oT_{tag}")
        rankr = ctx.sb([1, T], F32); rrk2 = ctx.fresh(f"rnk{tag}")
        def frnk(eng):
            return eng.tensor_tensor_scan(rankr[:, :], onesT[:, :], maskr[:, :], 0.0,
                                          ALU.mult, ALU.add)
        p.op("dve", frnk, reads={oTk, mrk2}, writes={rrk2}, tag=f"rnk_{tag}")
        Fr = ctx.sb([1, T], F32); Frk = ctx.fresh(f"F{tag}")
        def fF1(eng):
            return eng.tensor_single_scalar(Fr[:, :], srow[:, :], -1.0, ALU.add)
        p.op("dve", fF1, reads={srk, "consts"}, writes={Frk}, tag=f"F1_{tag}")
        def fF2(eng):
            return eng.tensor_tensor(Fr[:, :], Fr[:, :], maskr[:, :], ALU.mult)
        p.op("dve", fF2, reads={Frk, mrk2}, writes={Frk}, tag=f"F2_{tag}")
        def fF3(eng):
            return eng.tensor_single_scalar(Fr[:, :], Fr[:, :], 1.0, ALU.add)
        p.op("dve", fF3, reads={Frk, "consts"}, writes={Frk}, tag=f"F3_{tag}")
        for tt in range(16):
            for srcrow, srck, dstc, dstk in ((rankr, rrk2, rank_tm, rtmk),
                                             (maskr, mrk2, mask_tm, mtmk)):
                bt_, btk_ = ctx.take_bank()
                def ftp(eng, tt=tt, srcrow=srcrow, bt_=bt_):
                    return eng.transpose(bt_[:, 0:1],
                                         srcrow[:, tt * 128:(tt + 1) * 128],
                                         eye128[0:1, 0:1])
                p.op("pe", ftp, reads={srck, eyek}, writes={btk_}, tag=f"tp_{tag}")
                def fcpt(eng, tt=tt, dstc=dstc, bt_=bt_):
                    return eng.tensor_copy(dstc[:, tt:tt + 1], bt_[:, 0:1])
                p.op("dve", fcpt, reads={btk_}, writes={dstk}, tag=f"tc_{tag}")
        for c in range(4):
            t0 = c * 512
            bF, bFk = ctx.take_bank()
            def fFb(eng, t0=t0, bF=bF):
                return eng.matmul(bF[:, 0:512], ctx.ones1_f32[:, :],
                                  Fr[:, t0:t0 + 512], start=True, stop=True)
            p.op("pe", fFb, reads={Frk, "ones1"}, writes={bFk}, tag=f"Fb_{tag}")
            def fFc(eng, t0=t0, bF=bF):
                return eng.tensor_copy(Fb[:, t0:t0 + 512], bF[:, 0:512])
            p.op("dve", fFc, reads={bFk}, writes={Fbk}, tag=f"Fc_{tag}")
        ctx.dump("srow", srow[:, :], {srk})
        ctx.dump("bis", bis[:, :], {bisk})
        ctx.dump("maskr", maskr[:, :], {mrk2})
        ctx.dump("rankr", rankr[:, :], {rrk2})
        ctx.dump("s128", s128[:, :], {s128k})
        ctx._scopes.pop()
        _sa.close()
        p.barrier()
        P_sb = ctx.sb([128, 16 * NQ], BF16); Pk = ctx.fresh(f"P{tag}")
        P32 = ctx.sb([128, 16 * NQ], F32); P32k = ctx.fresh(f"P32{tag}")
        for tt in range(16):
            def fP(eng, tt=tt):
                return eng.tensor_scalar(P_sb[:, tt * NQ:(tt + 1) * NQ], qiota[:, :],
                                         rank_tm[:, tt:tt + 1],
                                         mask_tm[:, tt:tt + 1], ALU.is_equal, ALU.mult)
            p.op("dve", fP, reads={cnk, rtmk, mtmk}, writes={Pk}, tag=f"P_{tag}")
            def fP2(eng, tt=tt):
                return eng.tensor_scalar(P32[:, tt * NQ:(tt + 1) * NQ], qiota[:, :],
                                         rank_tm[:, tt:tt + 1],
                                         mask_tm[:, tt:tt + 1], ALU.is_equal, ALU.mult)
            p.op("dve", fP2, reads={cnk, rtmk, mtmk}, writes={P32k}, tag=f"P2_{tag}")
        EM = ctx.sb([128, 2 * NQ], BF16); EMk = ctx.fresh(f"EM{tag}")
        _sc2 = ExitStack()
        ctx._scopes.append(_sc2)
        T1t = ctx.sb([128, 16 * NQ], BF16); T1k = ctx.fresh(f"T1{tag}")
        ctx._scopes.pop()
        def evac_t1(oabs, t0, tw, pap, bk):
            ot = oabs // 128
            def f(eng, ot=ot, pap=pap):
                return eng.tensor_copy(T1t[:, ot * NQ:(ot + 1) * NQ], pap)
            p.op("dve", f, reads={bk}, writes={T1k}, tag=f"t1_{tag}")
        emit_linear(ctx, T, T, W["Mask"],
                    lambda kt, t0, tw: P_sb[:, kt * NQ + t0:kt * NQ + t0 + tw],
                    lambda kt: {Pk}, evac_t1, f"mg{tag}", tw=NQ, ntok=NQ)
        def evac_ms(oabs, t0, tw, pap, bk):
            ot = oabs // 128
            def f(eng, ot=ot, pap=pap):
                return eng.activation(EM[:, ot * NQ:(ot + 1) * NQ], pap, AF.Exp)
            p.op("act", f, reads={bk}, writes={EMk}, tag=f"em_{tag}")
            if ot == 1:
                def f2(eng, ot=ot):
                    return eng.tensor_tensor(EM[:, ot * NQ:(ot + 1) * NQ],
                                             EM[:, ot * NQ:(ot + 1) * NQ], km[:, :],
                                             ALU.mult)
                p.op("dve", f2, reads={EMk, cnk}, writes={EMk}, tag=f"em2_{tag}")
        emit_linear(ctx, T, NQ, W["Mask"],
                    lambda kt, t0, tw: T1t[:, kt * NQ + t0:kt * NQ + t0 + tw],
                    lambda kt: {T1k}, evac_ms, f"ms{tag}", tw=NQ, ntok=NQ,
                    w_sb=(P_sb, Pk))
        _sc2.close()
        p.barrier()
        cs_sel = ctx.sb([128, NQ], BF16); csk = ctx.fresh(f"cs{tag}")
        def evac_cs(oabs, t0, tw, pap, bk):
            def f(eng, pap=pap):
                return eng.tensor_copy(cs_sel[:, :], pap)
            p.op("dve", f, reads={bk}, writes={csk}, tag=f"cs_{tag}")
        emit_linear(ctx, T, 128, W["Tab"],
                    lambda kt, t0, tw: P_sb[:, kt * NQ + t0:kt * NQ + t0 + tw],
                    lambda kt: {Pk}, evac_cs, f"tb{tag}", tw=NQ, ntok=NQ)
        cos2s = ctx.sb([128, NQ], BF16); sin2s = ctx.sb([128, NQ], BF16)
        c2k = ctx.fresh(f"c2{tag}")
        for dst, lo in ((cos2s, 0), (sin2s, 64)):
            def fc1(eng, dst=dst, lo=lo):
                return eng.tensor_copy(dst[0:64, :], cs_sel[lo:lo + 64, :])
            p.op("dve", fc1, reads={csk}, writes={c2k}, tag=f"c2a_{tag}")
            def fc2(eng, dst=dst, lo=lo):
                return eng.tensor_copy(dst[64:128, :], cs_sel[lo:lo + 64, :])
            p.op("dve", fc2, reads={csk}, writes={c2k}, tag=f"c2b_{tag}")
        xs = ctx.sb([128, NKT * NQ], BF16); xsk = ctx.fresh(f"xs{tag}")
        _sb2 = ExitStack()
        ctx._scopes.append(_sb2)
        xn_tm = ctx.sb([128, 16 * 1024], BF16); xtmk = ctx.fresh(f"xtm{tag}")
        ctx._scopes.pop()
        for kt in range(NKT):
            for tt in range(16):
                bb, bbk = ctx.take_bf_bank()
                def ftx(eng, kt=kt, tt=tt, bb=bb):
                    return eng.transpose(bb[:, 0:128],
                                         xn[:, kt * T + tt * 128:kt * T + tt * 128 + 128],
                                         eye128b[:, :])
                p.op("pe", ftx, reads={xnk, ey128bk}, writes={bbk}, tag=f"tx_{tag}")
                def fcx(eng, kt=kt, tt=tt, bb=bb):
                    return eng.tensor_copy(
                        xn_tm[:, tt * 1024 + kt * 128:tt * 1024 + kt * 128 + 128],
                        bb[:, 0:128])
                p.op("dve", fcx, reads={bbk}, writes={xtmk}, tag=f"cx_{tag}")
        def evac_xs(oabs, t0, tw, pap, bk):
            ot = oabs // 128
            def f(eng, ot=ot, pap=pap):
                return eng.tensor_copy(xs[:, ot * NQ:(ot + 1) * NQ], pap)
            p.op("dve", f, reads={bk}, writes={xsk}, tag=f"xs_{tag}")
        emit_linear(ctx, T, D, None,
                    lambda kt, t0, tw: P_sb[:, kt * NQ + t0:kt * NQ + t0 + tw],
                    lambda kt: {Pk}, evac_xs, f"xg{tag}", tw=NQ, ntok=NQ,
                    w_sb=(xn_tm, xtmk))
        _sb2.close()
        p.barrier()
        ctx.dump("xs", xs[:, :], {xsk})
        ctx.dump("EM", EM[:, :], {EMk})
        ctx.dump("csel", cs_sel[:, :], {csk})
        def rms_norm(src, srck, gcols, gck, outb, outk, tg):
            x2c = ctx.scratch_bf
            for kt in range(NKT):
                def fsq(eng, kt=kt, src=src):
                    return eng.activation(x2c[:, kt * NQ:(kt + 1) * NQ],
                                          src[:, kt * NQ:(kt + 1) * NQ], AF.Square)
                p.op("act", fsq, reads={srck}, writes={"scrbf"}, tag=f"rq_{tg}")
            bss, bssk = ctx.take_bank()
            for kt in range(NKT):
                def fs2(eng, kt=kt, bss=bss):
                    return eng.matmul(bss[0:1, 0:NQ], ctx.onesc_bf[:, :],
                                      x2c[:, kt * NQ:(kt + 1) * NQ],
                                      start=(kt == 0), stop=(kt == NKT - 1))
                p.op("pe", fs2, reads={"scrbf", "ones1"}, writes={bssk}, tag=f"rs_{tg}")
            rstd = ctx.sb([1, NQ], F32); rsk = ctx.fresh(f"rst{tg}")
            def fr1(eng, bss=bss):
                return eng.tensor_single_scalar(rstd[:, :], bss[0:1, 0:NQ], 1.0 / D,
                                                ALU.mult)
            p.op("dve", fr1, reads={bssk, "consts"}, writes={rsk}, tag=f"r1_{tg}")
            def fr2(eng):
                return eng.activation(rstd[:, :], rstd[:, :], AF.Ln, bias=1e-6)
            p.op("act", fr2, reads={rsk, "consts"}, writes={rsk}, tag=f"r2_{tg}")
            def fr3(eng):
                return eng.activation(rstd[:, :], rstd[:, :], AF.Exp, scale=-0.5)
            p.op("act", fr3, reads={rsk}, writes={rsk}, tag=f"r3_{tg}")
            brs, brsk = ctx.take_bank()
            def fr4(eng, brs=brs):
                return eng.matmul(brs[:, 0:NQ], ctx.ones1_f32[:, :], rstd[:, :],
                                  start=True, stop=True)
            p.op("pe", fr4, reads={rsk, "ones1"}, writes={brsk}, tag=f"r4_{tg}")
            for kt in range(NKT):
                def fr5(eng, kt=kt, brs=brs, src=src):
                    return eng.tensor_tensor(ctx.scratch_f32[:, 0:NQ],
                                             src[:, kt * NQ:(kt + 1) * NQ],
                                             brs[:, 0:NQ], ALU.mult)
                p.op("dve", fr5, reads={srck, brsk}, writes={"scr32"}, tag=f"r5_{tg}")
                def fr6(eng, kt=kt, outb=outb, gcols=gcols):
                    return eng.tensor_scalar(outb[:, kt * NQ:(kt + 1) * NQ],
                                             ctx.scratch_f32[:, 0:NQ],
                                             gcols[:, kt:kt + 1], None, ALU.mult)
                p.op("dve", fr6, reads={"scr32", gck}, writes={outk}, tag=f"r6_{tg}")
        rn = ctx.sb([128, NKT * NQ], BF16); rnk = ctx.fresh(f"rn{tag}")
        rms_norm(xs, xsk, ln1c, l1k, rn, rnk, f"rm1{tag}")
        q_sel = ctx.sb([128, NKT * NQ], BF16); qsk = ctx.fresh(f"qs{tag}")
        k_sel = ctx.sb([128, NKT * NQ], BF16); ksk = ctx.fresh(f"ks{tag}")
        v_tms = ctx.sb([128, 2 * 16 * 65], BF16); vtk = ctx.fresh(f"vt{tag}")
        def fvon(eng):
            return eng.memset(
                v_tms[:, :].rearrange("p (tt h c) -> p tt h c", tt=2, h=16)[:, :, :, 64:65],
                1.0)
        p.op("dve", fvon, reads=set(), writes={vtk}, tag=f"vo_{tag}")
        scr2b = ctx.sb([128, NQ], F32); scr2bk = ctx.fresh(f"sc2{tag}")
        def rope_evac_s(dst, dstk, ot, t0, tw, pap, bk):
            scr = ctx.scratch_f32
            def s1(eng, pap=pap, tw=tw):
                return eng.tensor_single_scalar(scr[0:32, 0:tw], pap[32:64, :], -1.0,
                                                ALU.mult)
            p.op("dve", s1, reads={bk, "consts"}, writes={"scr32"}, tag=f"q1_{tag}")
            def s2(eng, pap=pap, tw=tw):
                return eng.tensor_copy(scr[32:64, 0:tw], pap[0:32, :])
            p.op("dve", s2, reads={bk}, writes={"scr32"}, tag=f"q2_{tag}")
            def s3(eng, pap=pap, tw=tw):
                return eng.tensor_single_scalar(scr[64:96, 0:tw], pap[96:128, :], -1.0,
                                                ALU.mult)
            p.op("dve", s3, reads={bk, "consts"}, writes={"scr32"}, tag=f"q3_{tag}")
            def s4(eng, pap=pap, tw=tw):
                return eng.tensor_copy(scr[96:128, 0:tw], pap[64:96, :])
            p.op("dve", s4, reads={bk}, writes={"scr32"}, tag=f"q4_{tag}")
            def p1(eng, pap=pap, tw=tw):
                return eng.tensor_tensor(scr2b[:, 0:tw], pap, cos2s[:, 0:tw], ALU.mult)
            p.op("dve", p1, reads={bk, c2k}, writes={scr2bk}, tag=f"q5_{tag}")
            def p2(eng, tw=tw, scr=scr):
                return eng.tensor_tensor(scr[:, 0:tw], scr[:, 0:tw], sin2s[:, 0:tw],
                                         ALU.mult)
            p.op("dve", p2, reads={"scr32", c2k}, writes={"scr32"}, tag=f"q6_{tag}")
            def p3(eng, ot=ot, tw=tw, scr=scr, dst=dst):
                return eng.tensor_tensor(dst[:, ot * NQ:ot * NQ + tw], scr2b[:, 0:tw],
                                         scr[:, 0:tw], ALU.add)
            p.op("dve", p3, reads={scr2bk, "scr32"}, writes={dstk}, tag=f"q7_{tag}")
        def evac_q(oabs, t0, tw, pap, bk):
            rope_evac_s(q_sel, qsk, oabs // 128, t0, tw, pap, bk)
        def evac_k(oabs, t0, tw, pap, bk):
            rope_evac_s(k_sel, ksk, oabs // 128, t0, tw, pap, bk)
        def evac_v(oabs, t0, tw, pap, bk):
            ot = oabs // 128
            def f(eng, pap=pap, tw=tw):
                return eng.tensor_copy(ctx.scratch_bf[:, 0:tw], pap)
            p.op("dve", f, reads={bk}, writes={"scrbf"}, tag=f"vv_{tag}")
            for hh in range(2):
                h = ot * 2 + hh
                for tkt in range(2):
                    bb, bbk = ctx.take_bf_bank()
                    def ftv(eng, hh=hh, tkt=tkt, bb=bb):
                        return eng.transpose(
                            bb[:, 0:64],
                            ctx.scratch_bf[hh * 64:hh * 64 + 64,
                                           tkt * 128:tkt * 128 + 128],
                            eye2b[hh * 64:hh * 64 + 64, :])
                    p.op("pe", ftv, reads={"scrbf", ey2k}, writes={bbk}, tag=f"tv_{tag}")
                    def fcv(eng, h=h, tkt=tkt, bb=bb):
                        return eng.tensor_copy(
                            v_tms[:, tkt * 1040 + h * 65:tkt * 1040 + h * 65 + 64],
                            bb[:, 0:64])
                    p.op("dve", fcv, reads={bbk}, writes={vtk}, tag=f"cv_{tag}")
        emit_linear(ctx, D, D, W["wq"],
                    lambda kt, t0, tw: rn[:, kt * NQ + t0:kt * NQ + t0 + tw],
                    lambda kt: {rnk}, evac_q, f"wq{tag}", tw=NQ, ntok=NQ)
        emit_linear(ctx, D, D, W["wk"],
                    lambda kt, t0, tw: rn[:, kt * NQ + t0:kt * NQ + t0 + tw],
                    lambda kt: {rnk}, evac_k, f"wk{tag}", tw=NQ, ntok=NQ)
        emit_linear(ctx, D, D, W["wv"],
                    lambda kt, t0, tw: rn[:, kt * NQ + t0:kt * NQ + t0 + tw],
                    lambda kt: {rnk}, evac_v, f"wv{tag}", tw=NQ, ntok=NQ)
        mqq = ctx.sb([128, 2], F32); mqqk = ctx.fresh(f"mqq{tag}")
        rowt = ctx.sb([1, 128], F32); rowtk = ctx.fresh(f"rwt{tag}")
        mr2 = ctx.sb([1, 4], F32); mr2k = ctx.fresh(f"mr2{tag}")
        negC = ctx.sb([128, 1], F32); nCk = ctx.fresh(f"nC{tag}")
        for si, (src_, sk_) in enumerate(((q_sel, qsk), (k_sel, ksk))):
            def fm_(eng, src_=src_):
                return eng.reduce_max(ctx.scratch_f32[:, 0:1], src_[:, :],
                                      axis=AX.X, apply_absolute_value=True)
            p.op("dve", fm_, reads={sk_}, writes={"scr32"}, tag=f"cm_{tag}")
            def fc_(eng, si=si):
                return eng.tensor_copy(mqq[:, si:si + 1], ctx.scratch_f32[:, 0:1])
            p.op("dve", fc_, reads={"scr32"}, writes={mqqk}, tag=f"cc2_{tag}")
            bt_, btk_ = ctx.take_bank()
            def ftp2(eng, si=si, bt_=bt_):
                return eng.transpose(bt_[0:1, 0:128], mqq[:, si:si + 1], eye128[:, :])
            p.op("pe", ftp2, reads={mqqk, eyek}, writes={btk_}, tag=f"ct_{tag}")
            def fcr2(eng, bt_=bt_):
                return eng.tensor_copy(rowt[:, :], bt_[0:1, 0:128])
            p.op("dve", fcr2, reads={btk_}, writes={rowtk}, tag=f"cr2_{tag}")
            def frd2(eng, si=si):
                return eng.reduce_max(mr2[:, si:si + 1], rowt[:, :], axis=AX.X)
            p.op("dve", frd2, reads={rowtk}, writes={mr2k}, tag=f"cd_{tag}")
        def fCx(eng):
            return eng.tensor_tensor(mr2[:, 2:3], mr2[:, 0:1], mr2[:, 1:2], ALU.mult)
        p.op("dve", fCx, reads={mr2k}, writes={mr2k}, tag=f"ce_{tag}")
        def fCy(eng):
            return eng.tensor_single_scalar(mr2[:, 2:3], mr2[:, 2:3], -8.0, ALU.mult)
        p.op("dve", fCy, reads={mr2k, "consts"}, writes={mr2k}, tag=f"cf_{tag}")
        bCx, bCxk = ctx.take_bank()
        def fCz(eng, bCx=bCx):
            return eng.matmul(bCx[:, 0:1], ctx.ones1_f32[:, :], mr2[:, 2:3],
                              start=True, stop=True)
        p.op("pe", fCz, reads={mr2k, "ones1"}, writes={bCxk}, tag=f"cg_{tag}")
        def fCw(eng, bCx=bCx):
            return eng.tensor_copy(negC[:, :], bCx[:, 0:1])
        p.op("dve", fCw, reads={bCxk}, writes={nCk}, tag=f"ch_{tag}")
        ao = ctx.sb([128, NKT * NQ], BF16); aok = ctx.fresh(f"ao{tag}")
        p_sb2 = ctx.sb([128, 2 * NQ], BF16); ps2k = ctx.fresh(f"p2{tag}")
        rro = ctx.sb([1, NQ], F32); rrok = ctx.fresh(f"rro{tag}")
        for h in range(16):
            hb = (h % 2) * 64
            hN = (h // 2) * NQ
            for tkt in range(2):
                psc, bk_ = ctx.take_bank()
                def fs_(eng, hb=hb, hN=hN, tkt=tkt, psc=psc):
                    return eng.matmul(psc[:, 0:NQ],
                                      k_sel[hb:hb + 64, hN + tkt * 128:hN + tkt * 128 + 128],
                                      q_sel[hb:hb + 64, hN:hN + NQ],
                                      start=True, stop=True)
                p.op("pe", fs_, reads={qsk, ksk}, writes={bk_}, tag=f"as_{tag}")
                def fe_(eng, tkt=tkt, psc=psc):
                    return eng.activation(ctx.scratch_bf[:, 0:NQ], psc[:, 0:NQ],
                                          AF.Exp, scale=0.125, bias=negC[:, 0:1])
                p.op("act", fe_, reads={bk_, nCk}, writes={"scrbf"}, tag=f"ae_{tag}")
                def fem(eng, tkt=tkt):
                    return eng.tensor_tensor(p_sb2[:, tkt * NQ:(tkt + 1) * NQ],
                                             ctx.scratch_bf[:, 0:NQ],
                                             EM[:, tkt * NQ:(tkt + 1) * NQ], ALU.mult)
                p.op("dve", fem, reads={"scrbf", EMk}, writes={ps2k}, tag=f"am_{tag}")
            ppv, bkpv = ctx.take_bank()
            for tkt in range(2):
                def fpv_(eng, h=h, tkt=tkt, ppv=ppv, st=(tkt == 0), sp=(tkt == 1)):
                    return eng.matmul(ppv[0:65, 0:NQ],
                                      v_tms[:, tkt * 1040 + h * 65:tkt * 1040 + h * 65 + 65],
                                      p_sb2[:, tkt * NQ:(tkt + 1) * NQ],
                                      start=st, stop=sp)
                p.op("pe", fpv_, reads={vtk, ps2k}, writes={bkpv}, tag=f"apv_{tag}")
            def fr_(eng, ppv=ppv):
                return eng.reciprocal(rro[:, :], ppv[64:65, 0:NQ])
            p.op("dve", fr_, reads={bkpv}, writes={rrok}, tag=f"ar_{tag}")
            pbc2, bkbc2 = ctx.take_bank()
            def fb_(eng, pbc2=pbc2):
                return eng.matmul(pbc2[0:64, 0:NQ], ctx.ones1_f32[:, 0:64], rro[:, :],
                                  start=True, stop=True)
            p.op("pe", fb_, reads={rrok, "ones1"}, writes={bkbc2}, tag=f"ab_{tag}")
            def fcpv(eng, ppv=ppv):
                return eng.tensor_copy(ctx.scratch_f32[0:64, 0:NQ], ppv[0:64, 0:NQ])
            p.op("dve", fcpv, reads={bkpv}, writes={"scr32"}, tag=f"ac_{tag}")
            def fdv_(eng, hb=hb, hN=hN, pbc2=pbc2):
                return eng.tensor_tensor(ao[hb:hb + 64, hN:hN + NQ],
                                         ctx.scratch_f32[0:64, 0:NQ],
                                         pbc2[0:64, 0:NQ], ALU.mult)
            p.op("dve", fdv_, reads={"scr32", bkbc2}, writes={aok}, tag=f"ad_{tag}")
        dsel = ctx.sb([128, NKT * NQ], BF16); dsk = ctx.fresh(f"ds{tag}")
        x1s = ctx.sb([128, NKT * NQ], BF16); x1k = ctx.fresh(f"x1{tag}")
        def evac_wo(oabs, t0, tw, pap, bk):
            ot = oabs // 128
            def f(eng, ot=ot, pap=pap):
                return eng.tensor_copy(dsel[:, ot * NQ:(ot + 1) * NQ], pap)
            p.op("dve", f, reads={bk}, writes={dsk}, tag=f"wo1_{tag}")
            def f2(eng, ot=ot, pap=pap):
                return eng.tensor_tensor(x1s[:, ot * NQ:(ot + 1) * NQ], pap,
                                         xs[:, ot * NQ:(ot + 1) * NQ], ALU.add)
            p.op("dve", f2, reads={bk, xsk}, writes={x1k}, tag=f"wo2_{tag}")
        emit_linear(ctx, D, D, W["wo"],
                    lambda kt, t0, tw: ao[:, kt * NQ + t0:kt * NQ + t0 + tw],
                    lambda kt: {aok}, evac_wo, f"wo{tag}", tw=NQ, ntok=NQ)
        rn2 = ctx.sb([128, NKT * NQ], BF16); rn2k = ctx.fresh(f"rn2{tag}")
        rms_norm(x1s, x1k, ln2c, l2k, rn2, rn2k, f"rm2{tag}")
        G = ctx.sb([128, 32 * NQ], BF16); Gk = ctx.fresh(f"G{tag}")
        def evac_g1(oabs, t0, tw, pap, bk):
            ot = oabs // 128
            def f(eng, ot=ot, pap=pap):
                return eng.activation(G[:, ot * NQ:(ot + 1) * NQ], pap,
                                      AF.Gelu_apprx_tanh)
            p.op("act", f, reads={bk}, writes={Gk}, tag=f"g1_{tag}")
        emit_linear(ctx, D, 4096, W["w1"],
                    lambda kt, t0, tw: rn2[:, kt * NQ + t0:kt * NQ + t0 + tw],
                    lambda kt: {rn2k}, evac_g1, f"w1{tag}", tw=NQ, ntok=NQ)
        def evac_g2(oabs, t0, tw, pap, bk):
            ot = oabs // 128
            def f(eng, ot=ot, pap=pap):
                sl = dsel[:, ot * NQ:(ot + 1) * NQ]
                return eng.tensor_tensor(sl, sl, pap, ALU.add)
            p.op("dve", f, reads={bk, dsk}, writes={dsk}, tag=f"g2_{tag}")
        emit_linear(ctx, 4096, D, W["w2"],
                    lambda kt, t0, tw: G[:, kt * NQ + t0:kt * NQ + t0 + tw],
                    lambda kt: {Gk}, evac_g2, f"w2{tag}", tw=NQ, ntok=NQ)
        wselr = ctx.sb([1, NQ], F32); wsk = ctx.fresh(f"ws{tag}")
        bws, bwsk = ctx.take_bank()
        for tt in range(16):
            def fws(eng, tt=tt, bws=bws):
                return eng.matmul(bws[0:1, 0:NQ], s128[:, tt:tt + 1],
                                  P32[:, tt * NQ:(tt + 1) * NQ],
                                  start=(tt == 0), stop=(tt == 15))
            p.op("pe", fws, reads={s128k, P32k}, writes={bwsk}, tag=f"ws_{tag}")
        def fws2(eng, bws=bws):
            return eng.tensor_copy(wselr[:, :], bws[0:1, 0:NQ])
        p.op("dve", fws2, reads={bwsk}, writes={wsk}, tag=f"ws2_{tag}")
        bwb, bwbk = ctx.take_bank()
        def fws3(eng, bwb=bwb):
            return eng.matmul(bwb[:, 0:NQ], ctx.ones1_f32[:, :], wselr[:, :],
                              start=True, stop=True)
        p.op("pe", fws3, reads={wsk, "ones1"}, writes={bwbk}, tag=f"ws3_{tag}")
        for kt in range(NKT):
            def fdw(eng, kt=kt, bwb=bwb):
                sl = dsel[:, kt * NQ:(kt + 1) * NQ]
                return eng.tensor_tensor(sl, sl, bwb[:, 0:NQ], ALU.mult)
            p.op("dve", fdw, reads={dsk, bwbk}, writes={dsk}, tag=f"dw_{tag}")
        ctx.dump("dsel", dsel[:, :], {dsk})
        ctx.dump("wsel", wselr[:, :], {wsk})
        ctx.dump("x1s", x1s[:, :], {x1k})
        d_tm = ctx.sb([128, 2 * 1024], BF16); dtmk = ctx.fresh(f"dtm{tag}")
        for kt in range(NKT):
            for qt in range(2):
                bb, bbk = ctx.take_bf_bank()
                def ftd(eng, kt=kt, qt=qt, bb=bb):
                    return eng.transpose(
                        bb[:, 0:128],
                        dsel[:, kt * NQ + qt * 128:kt * NQ + qt * 128 + 128],
                        eye128b[:, :])
                p.op("pe", ftd, reads={dsk, ey128bk}, writes={bbk}, tag=f"td_{tag}")
                def fcd(eng, kt=kt, qt=qt, bb=bb):
                    return eng.tensor_copy(
                        d_tm[:, qt * 1024 + kt * 128:qt * 1024 + kt * 128 + 128],
                        bb[:, 0:128])
                p.op("dve", fcd, reads={bbk}, writes={dtmk}, tag=f"cd_{tag}")
        Pt = ctx.sb([128, 2 * T], BF16); Ptk = ctx.fresh(f"Pt{tag}")
        for tt in range(16):
            for qt in range(2):
                bb, bbk = ctx.take_bf_bank()
                def ftq(eng, tt=tt, qt=qt, bb=bb):
                    return eng.transpose(
                        bb[:, 0:128],
                        P_sb[:, tt * NQ + qt * 128:tt * NQ + qt * 128 + 128],
                        eye128b[:, :])
                p.op("pe", ftq, reads={Pk, ey128bk}, writes={bbk}, tag=f"tq_{tag}")
                def fcq(eng, tt=tt, qt=qt, bb=bb):
                    return eng.tensor_copy(
                        Pt[:, qt * T + tt * 128:qt * T + tt * 128 + 128], bb[:, 0:128])
                p.op("dve", fcq, reads={bbk}, writes={Ptk}, tag=f"cq_{tag}")
        outc = ctx.sb([128, 512], F32); outck = ctx.fresh(f"oc{tag}")
        for ot in range(NKT):
            for tc in range(4):
                t0 = tc * 512
                psx, bkx = ctx.take_bank()
                for qt in range(2):
                    def fsc_(eng, ot=ot, qt=qt, t0=t0, psx=psx, st=(qt == 0),
                             sp=(qt == 1)):
                        return eng.matmul(
                            psx[:, 0:512],
                            d_tm[:, qt * 1024 + ot * 128:qt * 1024 + ot * 128 + 128],
                            Pt[:, qt * T + t0:qt * T + t0 + 512], start=st, stop=sp)
                    p.op("pe", fsc_, reads={dtmk, Ptk}, writes={bkx}, tag=f"sx_{tag}")
                def fo1(eng, ot=ot, t0=t0):
                    return eng.tensor_tensor(outc[:, :],
                                             xn[:, ot * T + t0:ot * T + t0 + 512],
                                             Fb[:, t0:t0 + 512], ALU.mult)
                p.op("dve", fo1, reads={xnk, Fbk}, writes={outck}, tag=f"o1_{tag}")
                def fo2(eng, psx=psx):
                    return eng.tensor_tensor(outc[:, :], outc[:, :], psx[:, 0:512],
                                             ALU.add)
                p.op("dve", fo2, reads={outck, bkx}, writes={outck}, tag=f"o2_{tag}")
                emit_resid_accum(ctx, outc[:, :], {outck}, ot, t0, 512, tag)


def _wdict_l0_ffn(inputs):
    L = "L0_ffn"
    mask = np.asarray(inputs[f"{L}_attention_mask"], np.float32)
    pos = np.asarray(inputs[f"{L}_position_ids"])
    inv = 1.0 / 10000.0 ** (np.arange(0, 64, 2, dtype=np.float64) / 64.0)
    ang = np.arange(4096, dtype=np.float64)[:, None] * inv[None, :]
    tab = np.concatenate([np.cos(ang), np.cos(ang), np.sin(ang), np.sin(ang)],
                         1).astype(np.float32)
    d = {
        f"{L}_router_w": prep_cols(np.asarray(inputs[f"{L}_router_w"],
                                              np.float32)[:, 0]),
        f"{L}_router_b": np.asarray(inputs[f"{L}_router_b"],
                                    np.float32).reshape(1, 1),
        f"{L}_wq": prep_wtiled(np.asarray(inputs[f"{L}_wq"], np.float32)),
        f"{L}_wk": prep_wtiled(np.asarray(inputs[f"{L}_wk"], np.float32)),
        f"{L}_wv": prep_wtiled(np.asarray(inputs[f"{L}_wv"], np.float32)),
        f"{L}_wo": prep_wtiled(np.asarray(inputs[f"{L}_wo"], np.float32)),
        f"{L}_w1": prep_wtiled(np.asarray(inputs[f"{L}_w1"], np.float32)),
        f"{L}_w2": prep_wtiled(np.asarray(inputs[f"{L}_w2"], np.float32)),
        f"{L}_ln1": prep_cols(np.asarray(inputs[f"{L}_ln1"], np.float32)),
        f"{L}_ln2": prep_cols(np.asarray(inputs[f"{L}_ln2"], np.float32)),
    }
    d["__percore__"] = {}
    for b in range(N_CORES):
        bb = min(b, 3)
        Mb = np.ascontiguousarray(mask[bb, 0, :T, :T])
        posr = pos[bb, :T]
        d["__percore__"][b] = {
            f"{L}_Mask": prep_wtiled(Mb, 128),
            f"{L}_Tab": prep_wtiled(np.ascontiguousarray(tab[posr]), 128),
        }
    return d


def _emit_l0_ffn(ctx, aps):
    L = "L0_ffn"
    W = {k.split("_", 2)[2]: aps[f"{L}_{k.split('_', 2)[2]}"] if False else None
         for k in []}
    W = {"router_w": aps[f"{L}_router_w"], "router_b": aps[f"{L}_router_b"],
         "wq": aps[f"{L}_wq"], "wk": aps[f"{L}_wk"], "wv": aps[f"{L}_wv"],
         "wo": aps[f"{L}_wo"], "w1": aps[f"{L}_w1"], "w2": aps[f"{L}_w2"],
         "ln1": aps[f"{L}_ln1"], "ln2": aps[f"{L}_ln2"],
         "Mask": aps[f"{L}_Mask"], "Tab": aps[f"{L}_Tab"],
         "c_qiota1": aps["c_qiota1"], "c_e255": aps["c_e255"],
         "c_neg1e9": aps["c_neg1e9"], "c_km": aps["c_km"],
         "c_eye128": aps["c_eye128"], "c_eye2b": aps["c_eye2b"],
         "c_eye128b": aps["c_eye128b"]}
    emit_l0_ffn(ctx, W)


BLOCKS["L0_ffn"] = (_wdict_l0_ffn, _emit_l0_ffn)



def _t(a):
    return np.ascontiguousarray(np.asarray(a, np.float32).T)


CONST_VALS = [1e-5, 1.0, 1e-6, 0.5, -1.0]

def host_constants():
    c = {}
    c["c_ones1"] = np.ones((1, 128), np.float32)
    c["c_onesc"] = np.ones((128, 1), np.float32)
    c["c_scal"] = np.tile(np.asarray(CONST_VALS, np.float32)[None, :], (128, 1))
    inv = 1.0 / 10000.0 ** (np.arange(0, 64, 2, dtype=np.float64) / 64.0)
    ang = np.arange(T, dtype=np.float64)[None, :] * inv[:, None]
    cosq = np.cos(ang).astype(np.float32)
    sinq = np.sin(ang).astype(np.float32)
    c["c_cos1"] = np.ascontiguousarray(np.vstack([cosq] * 4))
    c["c_sin1"] = np.ascontiguousarray(np.vstack([sinq] * 4))
    c["c_eye2b"] = np.vstack([np.eye(64)] * 2).astype(BF)
    c["c_eye128"] = np.eye(128, dtype=np.float32)
    tri = np.zeros((128, 4 * 512), np.float32)
    for m in range(4):
        i = np.arange(128)[:, None]
        j = np.arange(512)[None, :]
        tri[:, m * 512:(m + 1) * 512] = (i + 128 * m <= j)
    c["c_trimask"] = tri.astype(BF)
    c["c_qiota1"] = np.tile(np.arange(1, 257, dtype=np.float32)[None, :], (128, 1))
    e255 = np.zeros((1, 128), np.float32); e255[0, 127] = 1.0
    c["c_e255"] = e255
    c["c_neg1e9"] = np.full((1, 256), -1e9, np.float32)
    km = np.ones((128, 256), np.float32); km[127, :] = 0.0
    c["c_km"] = km.astype(BF)
    c["c_eye128b"] = np.eye(128).astype(BF)
    return c





def _wdict_channel_mix(L):
    def spec(inputs):
        return {
            f"{L}_W_r": prep_wtiled(_t(inputs[f"{L}_W_r"])),
            f"{L}_W_k": prep_wtiled(_t(inputs[f"{L}_W_k"])),
            f"{L}_W_v": prep_wtiled(_t(inputs[f"{L}_W_v"])),
            f"{L}_mu_r": prep_cols(np.asarray(inputs[f"{L}_mu_r"], np.float32)),
            f"{L}_mu_k": prep_cols(np.asarray(inputs[f"{L}_mu_k"], np.float32)),
        }
    return spec


def _emit_channel_mix_factory(L):
    def emit(ctx, aps):
        W = {"W_r": aps[f"{L}_W_r"], "W_k": aps[f"{L}_W_k"], "W_v": aps[f"{L}_W_v"],
             "mu_r": aps[f"{L}_mu_r"], "mu_k": aps[f"{L}_mu_k"]}
        emit_channel_mix(ctx, W, L)
    return emit


BLOCKS["L1_ffn"] = (_wdict_channel_mix("L1_ffn"), _emit_channel_mix_factory("L1_ffn"))
BLOCKS["L2_ffn"] = (_wdict_channel_mix("L2_ffn"), _emit_channel_mix_factory("L2_ffn"))

BLOCK_ORDER = ["L0_attn", "L0_ffn", "L1_attn", "L1_ffn",
               "L2_attn", "L2_ffn", "L3_attn", "L3_ffn"]


def build_program(block_names, weight_arrays, const_arrays):
    nc = bass.Bass("TRN2", target_bir_lowering=False)
    x_in = nc.dram_tensor("x_in", [D, T], F32, kind="ExternalInput").ap()
    x_out = nc.dram_tensor("x_out", [D, T], F32, kind="ExternalOutput").ap()
    aps = {}
    for name, arr in weight_arrays.items():
        dt = I32 if arr.dtype == np.int32 else F32
        aps[name] = nc.dram_tensor(name, list(arr.shape), dt, kind="ExternalInput").ap()
    for name, arr in const_arrays.items():
        dt = BF16 if arr.dtype == BF else F32
        aps[name] = nc.dram_tensor(name, list(arr.shape), dt, kind="ExternalInput").ap()
    prog = Prog(nc)
    with ExitStack() as stack:
        ctx = Ctx(nc, prog, stack)
        ctx.init_psum()
        ctx.scratch_f32 = ctx.sb([128, 512], F32, name="scr32")
        ctx.scratch_bf = ctx.sb([128, 2048], BF16, name="scrbf")
        ctx.rowm = ctx.sb([1, T], F32, name="rowm")
        ctx.rowv = ctx.sb([1, T], F32, name="rowv")
        ctx.wslots = [(ctx.sb([128, 4096], BF16, name=f"wslot{i}"), f"wslot{i}")
                      for i in range(2)]
        ctx.ones1_f32 = ctx.sb([1, 128], F32, name="ones1f")
        ctx.ones1_bf = ctx.sb([1, 128], BF16, name="ones1b")
        ctx.onesc_f32 = ctx.sb([128, 1], F32, name="onescf")
        ctx.onesc_bf = ctx.sb([128, 1], BF16, name="onescb")

        p = prog
        def ld_ones(eng):
            return eng.dma_start(out=ctx.ones1_f32[:, :], in_=aps["c_ones1"][:, :])
        p.dma("sp", ld_ones, reads=set(), writes={"ones1"}, tag="ld_ones")
        def ld_onesc(eng):
            return eng.dma_start(out=ctx.onesc_f32[:, :], in_=aps["c_onesc"][:, :])
        p.dma("sp", ld_onesc, reads=set(), writes={"ones1"}, tag="ld_onesc")
        def cvt_ones(eng):
            return eng.tensor_copy(ctx.ones1_bf[:, :], ctx.ones1_f32[:, :])
        p.op("dve", cvt_ones, reads={"ones1"}, writes={"ones1"}, tag="cvt_ones")
        def cvt_onesc(eng):
            return eng.tensor_copy(ctx.onesc_bf[:, :], ctx.onesc_f32[:, :])
        p.op("dve", cvt_onesc, reads={"ones1"}, writes={"ones1"}, tag="cvt_onesc")
        ctx.constcols = ctx.sb([128, len(CONST_VALS)], F32, name="constcols")
        def ld_consts(eng):
            return eng.dma_start(out=ctx.constcols[:, :], in_=aps["c_scal"][:, :])
        p.dma("sp", ld_consts, reads=set(), writes={"ones1"}, tag="ld_consts")
        for ci, cv in enumerate(CONST_VALS):
            nc.const_aps.register(F32, cv, ctx.constcols[:, ci:ci + 1])
        x_state = nc.dram_tensor("x_state", [D, T], F32, kind="Internal").ap()
        ctx.x_hbm = x_state
        allx = set(xkeys(0, T))
        def ld_x(eng):
            return eng.dma_start(out=x_state[:, :], in_=x_in[:, :])
        p.dma("pool", ld_x, reads=set(), writes=allx, tag="ld_x")
        for bn in block_names:
            BLOCKS[bn][1](ctx, aps)
        def st_x(eng):
            return eng.dma_start(out=x_out[:, :], in_=x_state[:, :])
        p.dma("pool", st_x, reads=allx, writes={"x_out_hbm"}, tag="st_x")
        def fin(eng):
            return eng.nop()
        p.op("sp", fin, reads={"x_out_hbm"}, writes=set(), tag="final_wait")
        stats = prog.emit(stack)
        stats["tags"] = [(i, o.engine + ("*" if o.dmaq else ""), o.tag)
                         for i, o in enumerate(prog.ops)]
    return nc, set(weight_arrays) | set(const_arrays), stats


def run_blocks(block_names, inputs, x_cur):
    weights = {}
    percore = [dict() for _ in range(N_CORES)]
    for bn in block_names:
        w = BLOCKS[bn][0](inputs)
        pc = w.pop("__percore__", None)
        weights.update(w)
        if pc:
            for b in range(N_CORES):
                percore[b].update(pc[b])
    decl = dict(weights)
    decl.update(percore[0])
    consts = host_constants()
    nc, innames, stats = build_program(block_names, decl, consts)
    in_maps = []
    for b in range(N_CORES):
        m = {"x_in": x_cur[b]}
        m.update(weights)
        m.update(percore[b])
        m.update(consts)
        in_maps.append(m)
    res = run_bass_kernel_spmd(nc, in_maps, core_ids=list(range(N_CORES)))
    return [np.ascontiguousarray(res.results[b]["x_out"]) for b in range(N_CORES)]


LAUNCH_GROUPS = [[b] for b in BLOCK_ORDER]


def kernel(**inputs):
    x = np.asarray(inputs["L0_attn_x"], np.float32)
    x_cur = [np.ascontiguousarray(x[b].T) for b in range(N_CORES)]
    for group in LAUNCH_GROUPS:
        x_cur = run_blocks(group, inputs, x_cur)
    return np.stack([xc.T for xc in x_cur], 0).astype(np.float32)
